# version 46
# baseline (speedup 1.0000x reference)
import contextlib
import numpy as np
import ml_dtypes
import concourse.bass as bass
import concourse.mybir as mybir
from concourse.bass_utils import run_bass_kernel_spmd

F32 = mybir.dt.float32
BF16 = mybir.dt.bfloat16
AF = mybir.ActivationFunctionType
Alu = mybir.AluOpType

NCORES = 8
L = 2048
D = 1024
NLT = 16
NHEAD = 16
DH = 64
CSTAB = 8.0
_PHASES = 4
_SUB = 99

ENGS = ("tensor", "vector", "scalar", "gpsimd", "sync")
G = "gpsimd"


def ts(i, n=128):
    return slice(i * n, (i + 1) * n)


class Prog:
    def __init__(self, sems):
        self.ops = {e: [] for e in ENGS}
        self.cnt = {name: 0 for name in sems}
        self.sems = sems

    def do(self, eng, fn, inc=None, by=1):
        if inc is not None:
            self.cnt[inc] += by
            sem = self.sems[inc]

            def run(e, fn=fn, sem=sem, n=by):
                fn(e).then_inc(sem, n)
            self.ops[eng].append(run)
            return self.cnt[inc]
        self.ops[eng].append(lambda e, fn=fn: fn(e))
        return None

    def wait(self, eng, sem, val=None):
        v = self.cnt[sem] if val is None else val
        if v and v > 0:
            s = self.sems[sem]
            self.ops[eng].append(lambda e, s=s, v=v: e.wait_ge(s, v))

    def mark(self, sem):
        return self.cnt[sem]


def build_nc():
    nc = bass.Bass("TRN2", target_bir_lowering=False)

    def din(name, shape, dtype=F32):
        return nc.dram_tensor(name, shape, dtype, kind="ExternalInput").ap()

    X = din("x", [L, D])
    WinT = din("w_inT_t", [24, 128, 8, 128])
    BinT = din("b_inT", [128, 24])
    WoutT = din("w_outT_t", [8, 128, 1024])
    BoutR = din("b_out_row", [1, D])
    GateW = din("gate_W_t", [128, 8, 8])
    W1t = din("W1_t", [8, 128, 4, 8, 128])
    B1T = din("b1T", [128, 32])
    W2t = din("W2_t", [8, 128, 4, 1024])
    B2E = din("b2ext", [9, D])
    Ws1t = din("Ws1_t", [128, 4, 8, 128])
    Bs1T = din("bs1T", [128, 4])
    Ws2t = din("Ws2_t", [128, 4, 1024])
    RS = din("route_scale", [1, 1])
    OnesR = din("ones_row", [1, L])
    Wqt = din("Wq_t", [8, 128, 8, 128])
    Wkvt = din("Wkv_t", [128, 8, 128])
    LnG = din("lngT", [128, 8])
    LnB = din("lnbT", [128, 8])
    WoM = din("Wo_m", [8, 128, 1024])
    OutLnG = din("out_ln_g", [1, D])
    OutLnB = din("out_ln_b", [1, D])
    NulKD = din("null_k_dup", [128, 1])
    VNulE = din("vnull_ext", [1, DH + 1])
    NabR = din("nab_row", [1, NHEAD])
    PbW0 = din("pb_w0", [1, D])
    PbW1t = din("pb_w1_t", [128, 8, 8, 128])
    PbW2t = din("pb_w2_t", [128, 8, 16])
    PbB0 = din("pb_b0T", [128, 8])
    PbL0G = din("pb_ln0_gT", [128, 8])
    PbL0B = din("pb_ln0_bT", [128, 8])
    PbB1 = din("pb_b1T", [128, 8])
    PbL1G = din("pb_ln1_gT", [128, 8])
    PbL1B = din("pb_ln1_bT", [128, 8])
    PbB2 = din("pb_b2c", [16, 1])
    B0R = din("pb_b0_row", [1, D])
    Ident = din("ident", [128, 128])
    PairSel = din("pairsel", [8, 512])
    SelE = din("selE", [9, 1024])

    OUT = nc.dram_tensor("out", [L, D], F32, kind="ExternalOutput").ap()

    def dscr(name, shape, dtype=BF16):
        return nc.dram_tensor(name, shape, dtype, kind="Internal")

    WinT_b = dscr("winT_b", [24, 128, 8, 128]).ap()
    WoutT_b = dscr("woutT_b", [8, 128, 1024]).ap()
    W1_b = dscr("w1_b", [8, 128, 4, 8, 128]).ap()
    W2_b = dscr("w2_b", [8, 128, 4, 1024]).ap()
    Ws1_b = dscr("ws1_b", [128, 4, 8, 128]).ap()
    Ws2_b = dscr("ws2_b", [128, 4, 1024]).ap()
    Wq_b = dscr("wq_b", [8, 128, 8, 128]).ap()
    Wkv_b = dscr("wkv_b", [128, 8, 128]).ap()
    Wo_b = dscr("wo_b", [8, 128, 1024]).ap()
    PbW1_b = dscr("pbw1_b", [128, 8, 8, 128]).ap()
    TBLh = dscr("tbl", [NHEAD, 4095])
    TBL = TBLh.ap()

    semnames = ["dX", "dC", "dPC", "dPC2", "dPC3", "dWo", "dSG2", "dSG3",
                "dU2", "dU3", "dSW2", "dSW3", "dT", "dO", "dK", "dR", "dP1", "dSG0", "dSG1",
                "dW0", "dW1", "dU0", "dU1", "dSW0", "dSW1", "dOD",
                "T", "V", "S", "Gs"]
    stack = contextlib.ExitStack()
    sems = {n: stack.enter_context(nc.semaphore(n)) for n in semnames}
    P = Prog(sems)

    def sbuf(name, shape, dtype):
        return stack.enter_context(nc.sbuf_tensor(name, shape, dtype))

    stream = sbuf("stream", [128, NLT, 1024], F32)
    xnT = sbuf("xnT", [128, 8, 2048], BF16)
    arA = sbuf("arA", [128, 24576], BF16)
    arB = sbuf("arB", [128, 8192], BF16)
    arC = sbuf("arC", [128, 10240], BF16)
    arD = sbuf("arD", [128, 6144], BF16)
    xn_tok = sbuf("xn_tok", [128, 2, 1024], BF16)
    ident = sbuf("identb", [128, 128], BF16)
    ident32f = sbuf("ident32f", [128, 128], F32)
    ones1 = sbuf("ones1", [1, 128], BF16)
    ones1f = sbuf("ones1f", [1, 128], F32)
    onesc = sbuf("onesc", [128, 1], BF16)
    onesT = sbuf("onesT", [128, 128], BF16)
    cm8 = sbuf("cm8", [128, 1], F32)
    epsb = sbuf("epsb", [128, 1], F32)
    lnscr = sbuf("lnscr", [128, 12], F32)
    bnst = sbuf("bnst", [128, 4, 6], F32)
    binT = sbuf("binT_s", [128, 24], F32)
    gatesb = sbuf("gatesb", [128, 8, 8], BF16)
    b1T = sbuf("b1T_s", [128, 32], F32)
    bs1T = sbuf("bs1T_s", [128, 4], F32)
    rsb = sbuf("rsb", [1, 1], F32)
    rs_b = sbuf("rs_b", [128, 1], F32)
    lngT = sbuf("lngT_s", [128, 8], F32)
    lnbTb = sbuf("lnbT_s", [128, 8], BF16)
    nabr = sbuf("nabr_s", [1, NHEAD], F32)
    nabm8 = sbuf("nabm8", [1, NHEAD], F32)
    nulkd = sbuf("nulkd_s", [128, 1], BF16)
    vnule = sbuf("vnule_s", [65, DH + 1], BF16)
    pbw2 = sbuf("pbw2_s", [128, 8, 16], BF16)
    pbc = sbuf("pbc", [128, 6, 8], F32)
    pbb2c = sbuf("pbb2c", [16, 1], F32)
    p0sc = sbuf("p0sc", [1, 8], F32)
    Sa = sbuf("Sa", [128, 128], F32)
    Ra = sbuf("Ra", [128, 128], F32)
    invq = sbuf("invq", [128, 8], F32)
    bkv = sbuf("bkv", [128, 1], F32)
    gwk = sbuf("gwk", [128, 24], F32)
    gtok = sbuf("gtok", [128, 8], BF16)

    psA_h = stack.enter_context(nc.psum_tensor("psA", [128, 2048], F32))
    psB_h = stack.enter_context(nc.psum_tensor("psB", [128, 2048], F32))
    psA = psA_h.ap()
    psB = psB_h.ap()
    psB_bf = psB_h.ap().bitcast(BF16)

    def cb(ar, off, shape):
        v = ar.ap()[:, off:off + int(np.prod(shape))]
        if len(shape) > 1:
            names = " ".join(f"d{i}" for i in range(len(shape)))
            v = v.rearrange(f"p ({names}) -> p {names}",
                            **{f"d{i}": s for i, s in enumerate(shape)})
        return v

    def cf(ar, off, shape):
        v = ar.ap().bitcast(F32)[:, off // 2:off // 2 + int(np.prod(shape))]
        if len(shape) > 1:
            names = " ".join(f"d{i}" for i in range(len(shape)))
            v = v.rearrange(f"p ({names}) -> p {names}",
                            **{f"d{i}": s for i, s in enumerate(shape)})
        return v

    xh1 = cb(arA, 0, [2, 8, 512])
    pbw1r = cb(arA, 8192, [8, 8, 128])
    posr = cb(arA, 16384, [4095])[0:1, :]
    tblsb = cb(arA, 20480, [4095])[0:16, :]
    h0fp = [cb(arB, 0, [8, 512]), cb(arB, 4096, [8, 512])]
    prodr = cf(arB, 0, [1024])[0:1, :]
    mtile = cb(arC, 0, [512])
    rtile = cb(arC, 512, [512])
    rowc = cf(arC, 1024, [3, 512])
    w0row = cb(arC, 5120, [1024])[0:1, :]
    b0row = cb(arC, 7168, [1024])[0:1, :]
    prs0 = cb(arD, 3072, [512])[0:1, :]
    prs1 = cb(arD, 3584, [512])[0:1, :]
    prs2 = cb(arD, 4096, [512])[0:1, :]
    mrowb = cb(arD, 4608, [1024])[0:1, :]
    p0sq = [cb(arC, 4096, [512]), cb(arC, 6144, [512])]
    mrow = cf(arD, 0, [1536])[0:1, :]
    xproj = cb(arA, 0, [12, 2048])
    kpack = cb(arB, 0, [4, 2048])
    woa = cb(arB, 0, [4, 1024])
    RhT = cb(arB, 4096, [2048])[0:8, :]
    wjbuf = cb(arC, 0, [2, 8, 128])
    escr = cb(arC, 2048, [2048])
    escr2 = cb(arC, 4096, [2048])
    hid = cb(arA, 0, [9, 4, 512])
    g_extT = cb(arD, 0, [2048])[0:9, :]
    qT = cb(arA, 0, [8, 2048])
    oddsl = cb(arA, 16384, [2048])
    kvb = cb(arB, 0, [2048])
    kdup = cb(arB, 2048, [2048])
    vtok = cb(arB, 4096, [16, DH + 1])
    enull_p = [cb(arB, 5200, [2048])[64:65, :],
               cb(arA, 16384, [2048])[64:65, :]]
    ywork = cf(arD, 0, [1024])
    invw = cf(arD, 2048, [2048])
    biased2v = cf(arD, 2048, [2048])
    optT = xnT
    boutR = cb(arB, 6144, [1024])[0:1, :]
    pairsel = cb(arB, 7168, [512])[0:8, :]
    selE = cb(arA, 18432, [1024])[0:9, :]
    b2e = cb(arA, 19456, [1024])[0:9, :]
    rbs = cf(arC, 0, [512])[0:64, :]
    invb = cb(arA, 18432, [2048])
    biased = cf(arA, 20480, [2048])
    SWBASE = 2048
    ETBASE = 6144

    def w1sl(buf, m, k):
        off = buf * 4096 + m * 1024 + k * 128
        return arC.ap()[:, off:off + 128]

    def w2sl(buf, hc, dc):
        off = buf * 4096 + hc * 1024 + dc * 512
        return arB.ap()[:, off:off + 512]

    def swv(buf, n):
        return arC.ap()[:, SWBASE + buf * 2048: SWBASE + buf * 2048 + n]

    def etv(buf, a, b):
        return arC.ap()[:, ETBASE + buf * 2048 + a: ETBASE + buf * 2048 + b]

    def sw_rev(buf, qc):
        off = SWBASE + buf * 2048 + (1536 - 512 * qc) + 511
        return bass.AP(arC, off, [[10240, 128], [-1, 512]])

    P.do(G, lambda e: e.iota(posr, [[-1, 4095]], base=2047,
                             channel_multiplier=0,
                             allow_small_or_imprecise_dtypes=True), inc="Gs")
    g_iota0 = P.mark("Gs")
    P.do(G, lambda e: e.dma_start(w0row[:], PbW0), inc="dPC2", by=16)
    P.do(G, lambda e: e.dma_start(b0row[:], B0R), inc="dPC2", by=16)
    P.do(G, lambda e: e.dma_start(pbw2[:], PbW2t), inc="dPC2", by=16)
    P.do(G, lambda e: e.dma_start(ident[:], Ident), inc="dPC2", by=16)
    pc_pb = P.mark("dPC2")
    for dst, src in [(gatesb[:], GateW),
                     (lnbTb[:], LnB), (nulkd[:], NulKD),
                     (vnule[64:65, :], VNulE)]:
        P.do(G, lambda e, d=dst, s=src: e.dma_start(d, s), inc="dPC2", by=16)
    pc_small = P.mark("dPC2")
    pbw1_in = P.do(G, lambda e: e.dma_start(
        pbw1r.rearrange("p a b c -> p (a b c)"),
        PbW1t.rearrange("p a b c -> p (a b c)")), inc="dP1", by=16)

    for dst, src in [(binT[:], BinT), (b1T[:], B1T), (bs1T[:], Bs1T),
                     (rsb[:], RS), (lngT[:], LnG), (nabr[:], NabR),
                     (pbc[:, 0, :], PbB0), (pbc[:, 1, :], PbL0G),
                     (pbc[:, 2, :], PbL0B), (pbc[:, 3, :], PbB1),
                     (pbc[:, 4, :], PbL1G), (pbc[:, 5, :], PbL1B),
                     (pbb2c[:], PbB2), (ident32f[:], Ident)]:
        P.do("sync", lambda e, d=dst, s=src: e.dma_start(d, s), inc="dC", by=16)
    consts_in = P.mark("dC")
    for lt in range(NLT):
        P.do("sync", lambda e, lt=lt: e.dma_start(
            stream[:, lt, :], X[ts(lt), :]), inc="dX", by=16)
    x_in = P.mark("dX")

    for t, v in [(ones1, 1.0), (ones1f, 1.0), (onesc, 1.0), (onesT, 1.0),
                 (cm8, -CSTAB), (epsb, 1e-5)]:
        P.do("vector", lambda e, t=t, v=v: e.memset(t[:], v))
    P.wait("vector", "dC", consts_in)
    v_nab = P.do("vector", lambda e: e.tensor_scalar(
        nabm8[:], nabr[:], -CSTAB, None, op0=Alu.add), inc="V")

    P.wait("tensor", "V", v_nab)
    tb1 = P.do("tensor", lambda e: e.matmul(
        psB[:, 0:1], ones1f[:], rsb[:], start=True, stop=True), inc="T")
    P.wait("vector", "T", tb1)
    consts_built = P.do("vector", lambda e: e.tensor_copy(
        rs_b[:], psB[:, 0:1]), inc="V")

    g_iota = g_iota0

    def ln_tile(lt):
        P.wait("vector", "dX", x_in)
        P.do("vector", lambda e, lt=lt: e.bn_stats(
            bnst[:, 0, :], stream[:, lt, 0:512]))
        v1 = P.do("vector", lambda e, lt=lt: e.bn_stats(
            bnst[:, 1, :], stream[:, lt, 512:1024]), inc="V")
        P.wait("vector", "V", v1)
        v2 = P.do("vector", lambda e: e.bn_aggr(
            lnscr[:, 0:2], bnst[:, 0:2, :].rearrange("p a b -> p (a b)")),
            inc="V")
        P.wait("scalar", "V", v2)
        s0 = P.do("scalar", lambda e: e.activation(
            lnscr[:, 2:3], lnscr[:, 1:2], AF.Ln, bias=epsb[:], scale=1.0),
            inc="S")
        P.wait("scalar", "S", s0)
        s1 = P.do("scalar", lambda e: e.activation(
            lnscr[:, 3:4], lnscr[:, 2:3], AF.Exp, scale=-0.5), inc="S")
        P.wait("vector", "S", s1)
        v3 = P.do("vector", lambda e, lt=lt: e.tensor_scalar(
            xn_tok[:, lt % 2, :], stream[:, lt, :], lnscr[:, 0:1],
            lnscr[:, 3:4], op0=Alu.subtract, op1=Alu.mult), inc="V")
        P.wait("tensor", "V", v3)
        base = 2048 + (lt % 2) * 1024
        for j in range(8):
            P.do("tensor", lambda e, lt=lt, j=j, base=base: e.transpose(
                psB_bf[:, base + j * 128:base + (j + 1) * 128],
                xn_tok[:, lt % 2, ts(j)], ident[:]), inc="T")
        t1 = P.mark("T")
        P.wait("vector", "T", t1)
        return P.do("vector", lambda e, lt=lt, base=base: e.tensor_copy(
            xnT[:, :, lt * 128:(lt + 1) * 128],
            psB_bf[:, base:base + 1024].rearrange("p (a c) -> p a c", a=8)),
            inc="V")

    def ln_tiles_pipe(lts=None):
        if lts is None:
            lts = list(range(NLT))
        am, bm, cm_, dm, vbm = {}, {}, {}, {}, {}

        def stA(lt, prev2=None):
            pa = lt % 2
            P.wait("vector", "dX", x_in)
            P.do("vector", lambda e, lt=lt, pa=pa: e.bn_stats(
                bnst[:, 2 * pa, :], stream[:, lt, 0:512]))
            v1 = P.do("vector", lambda e, lt=lt, pa=pa: e.bn_stats(
                bnst[:, 2 * pa + 1, :], stream[:, lt, 512:1024]), inc="V")
            P.wait("vector", "V", v1)
            if prev2 is not None:
                P.wait("vector", "S", cm_[prev2])
            am[lt] = P.do("vector", lambda e, pa=pa: e.bn_aggr(
                lnscr[:, 4 * pa:4 * pa + 2],
                bnst[:, 2 * pa:2 * pa + 2, :].rearrange("p a b -> p (a b)")),
                inc="V")

        def stB(lt, prev2=None):
            pa = lt % 2
            P.wait("scalar", "V", am[lt])
            if prev2 is not None:
                P.wait("scalar", "V", vbm[prev2])
            s0 = P.do("scalar", lambda e, pa=pa: e.activation(
                lnscr[:, 4 * pa + 2:4 * pa + 3], lnscr[:, 4 * pa + 1:
                                                       4 * pa + 2],
                AF.Ln, bias=epsb[:], scale=1.0), inc="S")
            P.wait("scalar", "S", s0)
            bm[lt] = P.do("scalar", lambda e, pa=pa: e.activation(
                lnscr[:, 4 * pa + 3:4 * pa + 4],
                lnscr[:, 4 * pa + 2:4 * pa + 3], AF.Exp, scale=-0.5),
                inc="S")

        def stC(lt):
            pa = lt % 2
            P.wait("vector", "S", bm[lt])
            vbm[lt] = P.do("vector", lambda e, pa=pa: e.scalar_tensor_tensor(
                lnscr[:, 4 * pa + 1:4 * pa + 2], lnscr[:, 4 * pa:4 * pa + 1],
                -1.0, lnscr[:, 4 * pa + 3:4 * pa + 4],
                op0=Alu.mult, op1=Alu.mult), inc="V")
            P.wait("scalar", "V", vbm[lt])
            cm_[lt] = P.do("scalar", lambda e, lt=lt, pa=pa: e.activation(
                xn_tok[:, pa, :], stream[:, lt, :], AF.Identity,
                bias=lnscr[:, 4 * pa + 1:4 * pa + 2],
                scale=lnscr[:, 4 * pa + 3:4 * pa + 4]), inc="S")

        def stD(lt):
            base = 2048 + (lt % 2) * 1024
            P.wait("tensor", "S", cm_[lt])
            for j in range(8):
                P.do("tensor", lambda e, lt=lt, j=j, base=base: e.transpose(
                    psB_bf[:, base + j * 128:base + (j + 1) * 128],
                    xn_tok[:, lt % 2, ts(j)], ident[:]), inc="T")
            dm[lt] = P.mark("T")

        def stE(lt):
            base = 2048 + (lt % 2) * 1024
            P.wait("vector", "T", dm[lt])
            return P.do("vector", lambda e, lt=lt, base=base: e.tensor_copy(
                xnT[:, :, lt * 128:(lt + 1) * 128],
                psB_bf[:, base:base + 1024].rearrange("p (a c) -> p a c",
                                                      a=8)), inc="V")

        stA(lts[0])
        stB(lts[0])
        last = 0
        for i, lt in enumerate(lts):
            stC(lt)
            stD(lt)
            if i + 1 < len(lts):
                stA(lts[i + 1], lts[i - 1] if i >= 1 else None)
                stB(lts[i + 1], lts[i - 1] if i >= 1 else None)
            last = stE(lt)
        return last

    P.wait("vector", "dPC2", pc_small)
    cch = P.do("vector", lambda e: e.tensor_tensor(
        prodr, w0row[:], w0row[:], Alu.mult), inc="V")
    P.wait("vector", "V", cch)
    cch = P.do("vector", lambda e: e.tensor_reduce(
        p0sc[0:1, 0:1], prodr, axis=mybir.AxisListType.X, op=Alu.add),
        inc="V")
    P.wait("vector", "V", cch)
    cch = P.do("vector", lambda e: e.tensor_tensor(
        prodr, w0row[:], b0row[:], Alu.mult), inc="V")
    P.wait("vector", "V", cch)
    cch = P.do("vector", lambda e: e.tensor_reduce(
        p0sc[0:1, 1:2], prodr, axis=mybir.AxisListType.X, op=Alu.add),
        inc="V")
    P.wait("vector", "V", cch)
    cch = P.do("vector", lambda e: e.tensor_tensor(
        prodr, b0row[:], b0row[:], Alu.mult), inc="V")
    P.wait("vector", "V", cch)
    cch = P.do("vector", lambda e: e.tensor_reduce(
        p0sc[0:1, 2:3], prodr, axis=mybir.AxisListType.X, op=Alu.add),
        inc="V")
    P.do("vector", lambda e: e.tensor_reduce(
        p0sc[0:1, 3:4], w0row[:], axis=mybir.AxisListType.X, op=Alu.add))
    cch = P.do("vector", lambda e: e.tensor_reduce(
        p0sc[0:1, 4:5], b0row[:], axis=mybir.AxisListType.X, op=Alu.add),
        inc="V")
    P.wait("vector", "V", cch)
    for sl, sc in [(0, 1.0 / 1024), (1, 2.0 / 1024), (2, 1.0 / 1024),
                   (3, 1.0 / 1024), (4, 1.0 / 1024)]:
        cch = P.do("vector", lambda e, sl=sl, sc=sc: e.tensor_scalar(
            p0sc[0:1, sl:sl + 1], p0sc[0:1, sl:sl + 1], sc, None,
            op0=Alu.mult), inc="V")
    p0c_done = P.mark("V")

    rowA = rowc[0:1, 0, :]
    rowB = rowc[0:1, 1, :]
    rowC = rowc[0:1, 2, :]

    def posmlp():
        P.wait("tensor", "dPC2", pc_small)
        P.wait("tensor", "dP1", pbw1_in)
        P.wait("tensor", "V", consts_built)
        P.wait("scalar", "dC", consts_in)
        P.wait("vector", "Gs", g_iota)
        n = 512
        tl2d = {}
        l0rd = {}
        evd = {}
        cpyd = {}
        l2evd = {}
        sqm = {0: 0, 1: 0}
        rowm_d = {}
        sx1_d = {}
        sx2_d = {}
        tst_d = {}
        nrm_d = {}
        bcd = {}

        def st_rows(rc):
            r0 = rc * 512
            pos_n = posr[0:1, r0:r0 + n]
            if rc > 0:
                P.wait("vector", "T", l0rd[rc - 1])
            else:
                P.wait("vector", "V", p0c_done)
            q1 = P.do("vector", lambda e, pos_n=pos_n: e.tensor_tensor(
                rowA[:, 0:n], pos_n, pos_n, Alu.mult), inc="V")
            P.do("vector", lambda e, pos_n=pos_n: e.tensor_scalar(
                rowB[:, 0:n], pos_n, p0sc[0:1, 1:2], p0sc[0:1, 2:3],
                op0=Alu.mult, op1=Alu.add))
            P.wait("vector", "V", q1)
            q3 = P.do("vector", lambda e: e.tensor_scalar(
                rowA[:, 0:n], rowA[:, 0:n], p0sc[0:1, 0:1], None,
                op0=Alu.mult), inc="V")
            P.wait("vector", "V", q3)
            q4 = P.do("vector", lambda e: e.tensor_tensor(
                rowB[:, 0:n], rowB[:, 0:n], rowA[:, 0:n], Alu.add), inc="V")
            q5 = P.do("vector", lambda e, pos_n=pos_n: e.tensor_scalar(
                rowC[:, 0:n], pos_n, p0sc[0:1, 3:4], p0sc[0:1, 4:5],
                op0=Alu.mult, op1=Alu.add), inc="V")
            P.wait("vector", "V", q5)
            q6 = P.do("vector", lambda e: e.scalar_tensor_tensor(
                rowA[:, 0:n], rowC[:, 0:n], 1.0, rowC[:, 0:n],
                op0=Alu.mult, op1=Alu.mult), inc="V")
            P.wait("vector", "V", q6)
            q7 = P.do("vector", lambda e: e.tensor_tensor(
                rowA[:, 0:n], rowB[:, 0:n], rowA[:, 0:n], Alu.subtract),
                inc="V")
            P.wait("scalar", "V", q7)
            sr1 = P.do("scalar", lambda e: e.activation(
                rowB[:, 0:n], rowA[:, 0:n], AF.Ln, bias=epsb[0:1, :],
                scale=1.0), inc="S")
            P.wait("scalar", "S", sr1)
            sr2 = P.do("scalar", lambda e: e.activation(
                rowA[:, 0:n], rowB[:, 0:n], AF.Exp, scale=-0.5), inc="S")
            P.wait("vector", "S", sr2)
            P.do("vector", lambda e, pos_n=pos_n: e.tensor_tensor(
                prs0[:, 0:n], pos_n, rowA[:, 0:n], Alu.mult))
            P.do("vector", lambda e: e.tensor_scalar(
                prs1[:, 0:n], rowA[:, 0:n], 1.0, None, op0=Alu.mult))
            rowm_d[rc] = P.do("vector", lambda e: e.scalar_tensor_tensor(
                prs2[:, 0:n], rowC[:, 0:n], -1.0, rowA[:, 0:n],
                op0=Alu.mult, op1=Alu.mult), inc="V")

        def st_l0(rc):
            par = rc % 2
            psX = psA if par == 0 else psB
            P.wait("tensor", "V", rowm_d[rc])
            if rc >= 2:
                P.wait("tensor", "V", evd[(rc - 2, 7)])
            sl0 = {}
            for fc in range(8):
                bank = fc % 2
                if fc >= 2:
                    P.wait("tensor", "S", sl0[fc - 2])
                P.do("tensor", lambda e, fc=fc, bank=bank, psX=psX:
                     e.matmul(psX[:, bank * 512:bank * 512 + n],
                              w0row[:, ts(fc)], prs0[:, 0:n],
                              start=True, stop=False), inc="T")
                P.do("tensor", lambda e, fc=fc, bank=bank, psX=psX:
                     e.matmul(psX[:, bank * 512:bank * 512 + n],
                              b0row[:, ts(fc)], prs1[:, 0:n],
                              start=False, stop=False), inc="T")
                P.do("tensor", lambda e, bank=bank, psX=psX:
                     e.matmul(psX[:, bank * 512:bank * 512 + n],
                              ones1[:], prs2[:, 0:n],
                              start=False, stop=True), inc="T")
                tm0 = P.mark("T")
                P.wait("scalar", "T", tm0)
                if fc == 0 and rc >= 2:
                    P.wait("scalar", "T", tl2d[rc - 2])
                sl0[fc] = P.do("scalar", lambda e, fc=fc, bank=bank,
                               psX=psX, par=par: e.activation(
                    xh1[:, par, fc, 0:n], psX[:, bank * 512:bank * 512 + n],
                    AF.Silu, bias=pbc[:, 2, fc:fc + 1],
                    scale=pbc[:, 1, fc:fc + 1]), inc="S")
            l0rd[rc] = P.mark("T")
            sx1_d[rc] = P.mark("S")

        def st_l1(rc):
            par = rc % 2
            psX = psA if par == 0 else psB
            h0 = h0fp[par]
            P.wait("tensor", "S", sx1_d[rc])
            for fc in range(8):
                bank = fc % 2
                if fc >= 2:
                    P.wait("tensor", "V", evd[(rc, fc - 2)])
                for k in range(8):
                    P.do("tensor", lambda e, fc=fc, k=k, bank=bank,
                         par=par, psX=psX: e.matmul(
                        psX[:, bank * 512:bank * 512 + n],
                        pbw1r[:, fc, k, :], xh1[:, par, k, 0:n],
                        start=(k == 0), stop=(k == 7)), inc="T")
                tmm = P.mark("T")
                P.wait("vector", "T", tmm)
                evd[(rc, fc)] = P.do("vector", lambda e, fc=fc,
                                     bank=bank, psX=psX, h0=h0:
                                     e.tensor_scalar(
                    h0[:, fc, 0:n], psX[:, bank * 512:bank * 512 + n],
                    pbc[:, 3, fc:fc + 1], None, op0=Alu.add), inc="V")

        def st_stats(rc):
            h0 = h0fp[rc % 2]
            for fc in range(8):
                pb2 = fc % 2
                P.wait("scalar", "V", evd[(rc, fc)])
                if fc >= 2:
                    P.wait("scalar", "T", sqm[pb2])
                sm = P.do("scalar", lambda e, fc=fc, pb2=pb2, h0=h0:
                          e.activation(p0sq[pb2][:, 0:n], h0[:, fc, 0:n],
                                       AF.Square), inc="S")
                if fc == 0:
                    P.wait("tensor", "V", cpyd.get(rc - 1, 0))
                    P.wait("tensor", "S", l2evd.get(rc - 1, 0))
                P.wait("tensor", "V", evd[(rc, fc)])
                P.do("tensor", lambda e, fc=fc, h0=h0: e.matmul(
                    psA[0:1, 1024:1024 + n], onesc[:], h0[:, fc, 0:n],
                    start=(fc == 0), stop=(fc == 7)), inc="T")
                P.wait("tensor", "S", sm)
                sqm[pb2] = P.do("tensor", lambda e, fc=fc, pb2=pb2:
                                e.matmul(
                    psA[0:1, 1536:1536 + n], onesc[:], p0sq[pb2][:, 0:n],
                    start=(fc == 0), stop=(fc == 7)), inc="T")
            tst_d[rc] = P.mark("T")

        def st_mrow(rc):
            P.wait("vector", "T", tst_d[rc])
            P.do("vector", lambda e: e.tensor_scalar(
                mrow[0:1, 0:n], psA[0:1, 1024:1024 + n], 1.0 / 1024.0, None,
                op0=Alu.mult))
            u1 = P.do("vector", lambda e: e.tensor_scalar(
                mrow[0:1, 512:512 + n], psA[0:1, 1536:1536 + n], 1.0 / 1024.0,
                None, op0=Alu.mult), inc="V")
            P.wait("vector", "V", u1)
            u2 = P.do("vector", lambda e: e.tensor_tensor(
                mrow[0:1, 1024:1024 + n], mrow[0:1, 0:n], mrow[0:1, 0:n],
                Alu.mult), inc="V")
            P.wait("vector", "V", u2)
            u3 = P.do("vector", lambda e: e.tensor_tensor(
                mrow[0:1, 512:512 + n], mrow[0:1, 512:512 + n],
                mrow[0:1, 1024:1024 + n], Alu.subtract), inc="V")
            P.wait("scalar", "V", u3)
            w1s = P.do("scalar", lambda e: e.activation(
                mrow[0:1, 1024:1024 + n], mrow[0:1, 512:512 + n], AF.Ln,
                bias=epsb[0:1, :], scale=1.0), inc="S")
            P.wait("scalar", "S", w1s)
            w2s = P.do("scalar", lambda e: e.activation(
                mrow[0:1, 512:512 + n], mrow[0:1, 1024:1024 + n], AF.Exp,
                scale=-0.5), inc="S")
            if rc >= 2:
                P.wait("vector", "dSW0", bcd[rc - 2])
            P.do("vector", lambda e, n=n: e.tensor_scalar(
                mrowb[0:1, 0:n], mrow[0:1, 0:n], 1.0, None, op0=Alu.mult))
            P.wait("vector", "S", w2s)
            cmr = P.do("vector", lambda e, n=n: e.tensor_scalar(
                mrowb[0:1, 512:512 + n], mrow[0:1, 512:512 + n], 1.0, None,
                op0=Alu.mult), inc="V")
            P.wait(G, "V", cmr)
            if rc >= 2:
                P.wait(G, "V", nrm_d[rc - 2])
            P.do(G, lambda e, n=n: e.dma_start(
                mtile[:, 0:n],
                bass.AP(arD, 4608, [[6144, 1], [0, 128], [1, n]])),
                inc="dSW0", by=16)
            bcd[rc] = P.do(G, lambda e, n=n: e.dma_start(
                rtile[:, 0:n],
                bass.AP(arD, 5120, [[6144, 1], [0, 128], [1, n]])),
                inc="dSW0", by=16)
            cpyd[rc] = P.mark("V")
            P.wait("vector", "dSW0", bcd[rc])

        def st_norm(rc):
            par = rc % 2
            h0 = h0fp[par]
            for fc in range(8):
                v3 = P.do("vector", lambda e, fc=fc, h0=h0:
                          e.tensor_tensor(h0[:, fc, 0:n], h0[:, fc, 0:n],
                                          mtile[:, 0:n], Alu.subtract),
                          inc="V")
                P.wait("vector", "V", v3)
                vn = P.do("vector", lambda e, fc=fc, h0=h0:
                          e.tensor_tensor(h0[:, fc, 0:n], h0[:, fc, 0:n],
                                          rtile[:, 0:n], Alu.mult), inc="V")
                P.wait("scalar", "V", vn)
                P.do("scalar", lambda e, fc=fc, h0=h0, par=par:
                     e.activation(xh1[:, par, fc, 0:n], h0[:, fc, 0:n],
                                  AF.Silu, bias=pbc[:, 5, fc:fc + 1],
                                  scale=pbc[:, 4, fc:fc + 1]), inc="S")
                nrm_d[rc] = vn
            sx2_d[rc] = P.mark("S")

        def st_l2(rc):
            r0 = rc * 512
            par = rc % 2
            P.wait("tensor", "S", sx2_d[rc])
            for k in range(8):
                P.do("tensor", lambda e, k=k, par=par: e.matmul(
                    psA[0:16, 1024:1024 + n], pbw2[:, k, :],
                    xh1[:, par, k, 0:n],
                    start=(k == 0), stop=(k == 7)), inc="T")
            tl2d[rc] = P.mark("T")
            P.wait("scalar", "T", tl2d[rc])
            l2evd[rc] = P.do("scalar", lambda e, r0=r0: e.activation(
                tblsb[0:16, r0:r0 + n], psA[0:16, 1024:1024 + n], AF.Identity,
                bias=pbb2c[0:16, :], scale=1.0), inc="S")

        st_rows(0)
        st_l0(0)
        for rc in range(4):
            st_l1(rc)
            if rc + 1 < 4:
                st_rows(rc + 1)
            st_stats(rc)
            if rc + 1 < 4:
                st_l0(rc + 1)
            st_mrow(rc)
            st_norm(rc)
            st_l2(rc)
            lt_sched = {0: list(range(0, 6)), 1: list(range(6, 12)),
                        2: list(range(12, 16)), 3: []}[rc]
            if lt_sched:
                ln_tiles_pipe(lts=lt_sched)
        vmsk = P.do("vector", lambda e: e.memset(
            tblsb[0:16, 2048:4095], -1e9), inc="V")
        stbl = P.mark("S")
        P.wait(G, "S", stbl)
        P.wait(G, "V", vmsk)
        P.do(G, lambda e: e.dma_start(TBL, tblsb), inc="dT", by=16)
    posmlp()
    pos_T = P.mark("T")
    pos_S = P.mark("S")
    pos_V = P.mark("V")
    tbl_done = P.mark("dT")

    def attn_a():
        P.wait("vector", "dX", x_in)
        P.wait("tensor", "V", consts_built)
        xnT_done = P.mark("V")
        if _SUB <= 1:
            return P.mark("V")
        P.wait("vector", "dT", tbl_done)

        P.wait(G, "V", pos_V)
        P.wait(G, "S", pos_S)
        res_prev = {}
        exp_last = {0: 0, 1: 0}
        for half in range(2):
            if half == 1:
                for d_, s_ in [(W1_b, W1t), (Ws1_b, Ws1t),
                               (W2_b, W2t), (Ws2_b, Ws2t)]:
                    P.do(G, lambda e, d_=d_, s_=s_: e.dma_start(d_, s_),
                         inc="dPC3", by=16)
            ev = {}
            mmdone = {}
            for jj in range(12):
                j = half * 12 + jj
                buf = jj % 2
                wsem = "dW0" if buf == 0 else "dW1"
                gidx = half * 12 + jj
                if gidx >= 2:
                    P.wait(G, "T", mmdone.get(gidx - 2, 0))
                wm = P.do(G, lambda e, j=j, buf=buf: e.dma_start(
                    wjbuf[:, buf, :, :], WinT[j]), inc=wsem, by=16)
                P.wait("tensor", wsem, wm)
                if jj == 0:
                    if half == 0:
                        P.wait("tensor", "V", xnT_done)
                        P.wait("tensor", "T", pos_T)
                        P.wait("tensor", "S", pos_S)
                    else:
                        P.wait("tensor", "S", exp_last[0])
                        P.wait("tensor", "S", exp_last[1])
                        P.wait("tensor", "V", P.mark("V"))
                for lsub in range(4):
                    pb = lsub % 2
                    if ("ev", pb) in ev:
                        P.wait("tensor", "V", ev[("ev", pb)])
                    for k in range(8):
                        P.do("tensor", lambda e, buf=buf, k=k, lsub=lsub,
                             pb=pb: e.matmul(
                            psB[:, pb * 512:pb * 512 + 512],
                            wjbuf[:, buf, k, :], xnT[:, k, ts(lsub, 512)],
                            start=(k == 0), stop=(k == 7)), inc="T")
                    tm = P.mark("T")
                    if lsub == 3:
                        mmdone[gidx] = tm
                    P.wait("vector", "T", tm)
                    em = P.do("vector", lambda e, j=j, jj=jj, lsub=lsub,
                              pb=pb: e.tensor_scalar(
                        xproj[:, jj, ts(lsub, 512)],
                        psB[:, pb * 512:pb * 512 + 512],
                        binT[:, j:j + 1], None, op0=Alu.add), inc="V")
                    ev[("ev", pb)] = em
                    if lsub == 3:
                        ev[("chunk", jj)] = em
            if _SUB <= 2:
                return P.mark("V")
            kp = {}
            for pp in range(4):
                c0 = 3 * pp
                P.wait(G, "V", ev[("chunk", c0)])
                P.wait(G, "V", ev[("chunk", c0 + 2)])
                P.do(G, lambda e, pp=pp, c0=c0: e.dma_start(
                    kpack[0:64, pp, :], xproj[64:128, c0, :]), inc="dK", by=16)
                kp[pp] = P.do(G, lambda e, pp=pp, c0=c0: e.dma_start(
                    kpack[64:128, pp, :], xproj[0:64, c0 + 2, :]),
                    inc="dK", by=16)
            kp_all = P.mark("dK")
            if _SUB <= 3:
                return P.mark("V")
            P.wait("tensor", "dK", kp_all)
            P.wait("tensor", "V", ev[("ev", 0)])
            P.wait("tensor", "V", ev[("ev", 1)])
            for pp in range(4):
                c0 = 3 * pp
                for qt in range(16):
                    for par in range(2):
                        hh = 2 * pp + par
                        qsl = c0 if par == 0 else c0 + 1
                        base = 0 if par == 0 else 64
                        ps = psA if par == 0 else psB
                        if exp_last[par]:
                            P.wait("tensor", "S", exp_last[par])
                        for mc in range(4):
                            P.do("tensor", lambda e, qsl=qsl, base=base,
                                 qt=qt, pp=pp, mc=mc, ps=ps: e.matmul(
                                ps[:, ts(mc, 512)],
                                xproj[base:base + 64, qsl, ts(qt)],
                                kpack[base:base + 64, pp, ts(mc, 512)],
                                start=True, stop=True), inc="T")
                        tm = P.mark("T")
                        P.wait("scalar", "T", tm)
                        edst = escr if par == 0 else escr2
                        exp_last[par] = P.do(
                            "scalar", lambda e, ps=ps, qt=qt, hh=hh,
                            edst=edst:
                            e.activation(edst, ps[:, :], AF.Exp, bias=cm8[:],
                                         scale=0.125,
                                         accum_out=Sa[:, hh * 16 + qt:
                                                      hh * 16 + qt + 1]),
                            inc="S")
            s_exp = P.mark("S")
            if _SUB <= 4:
                return P.mark("V")

            P.wait("vector", "S", s_exp)
            v1 = P.do("vector", lambda e: e.reciprocal(Ra[:], Sa[:]), inc="V")
            P.wait("vector", "V", v1)
            v2 = P.do("vector", lambda e: e.tensor_tensor(
                Ra[:], Sa[:], Ra[:], Alu.mult), inc="V")
            P.wait("tensor", "V", v2)
            for qt in range(16):
                if qt == 8:
                    P.wait("tensor", "V", P.mark("V"))
                src = bass.AP(Ra, qt, [[128, 128], [16, 8]])
                P.do("tensor", lambda e, qt=qt, src=src: e.transpose(
                    psA[0:8, (qt % 8) * 128:(qt % 8) * 128 + 128], src,
                    ident32f[:]), inc="T")
                if qt % 8 == 7:
                    trm = P.mark("T")
                    P.wait("vector", "T", trm)
                    for q2 in range(qt - 7, qt + 1):
                        P.do("vector", lambda e, q2=q2: e.tensor_copy(
                            RhT[0:8, ts(q2)],
                            psA[0:8, (q2 % 8) * 128:(q2 % 8) * 128 + 128]),
                            inc="V")
            rat = P.mark("V")

            P.wait(G, "S", s_exp)
            P.do(G, lambda e: e.dma_start(pairsel, PairSel), inc="dK", by=16)
            if half == 0:
                P.do(G, lambda e: e.dma_start(boutR, BoutR), inc="dK", by=16)
            ab_consts = P.mark("dK")
            P.wait("tensor", "dK", ab_consts)
            vs = {}
            cpq2 = {0: P.mark("V"), 1: rat}
            for lc in range(4):
                c_ready = cpq2[0] if lc < 2 else rat
                for pp in range(4):
                    c0 = 3 * pp
                    idx = lc * 4 + pp
                    bank = idx % 2
                    if idx == 0:
                        P.wait("tensor", "V", cpq2[0])
                    if idx == 8:
                        P.wait("tensor", "V", rat)
                    if idx >= 2:
                        P.wait("tensor", "V", vs.get(idx - 2, 0))
                    P.do("tensor", lambda e, pp=pp, bank=bank, lc=lc:
                         e.matmul(psB[:, bank * 512:bank * 512 + 512],
                                  pairsel[0:8, ts(pp)], RhT[0:8, ts(lc, 512)],
                                  start=True, stop=True), inc="T")
                    tm = P.mark("T")
                    P.wait("vector", "T", tm)
                    P.do("vector", lambda e, c0=c0, lc=lc, bank=bank:
                         e.tensor_tensor(
                        xproj[0:64, c0, ts(lc, 512)],
                        xproj[0:64, c0 + 1, ts(lc, 512)],
                        psB[0:64, bank * 512:bank * 512 + 512], Alu.mult),
                        inc="V")
                    vs[idx] = P.do(
                        "vector", lambda e, c0=c0, lc=lc, bank=bank:
                        e.tensor_tensor(
                            xproj[64:128, c0, ts(lc, 512)],
                            xproj[64:128, c0 + 2, ts(lc, 512)],
                            psB[64:128, bank * 512:bank * 512 + 512],
                            Alu.mult), inc="V")
            vsc_done = P.mark("V")
            if _SUB <= 5:
                return P.mark("V")

            P.wait(G, "S", s_exp)
            wm = P.do(G, lambda e, half=half: e.dma_start(
                woa, WoutT[4 * half:4 * half + 4]
                .rearrange("a p c -> p a c")), inc="dR", by=16)
            P.wait("tensor", "dR", wm)
            for ltg in range(4):
                for lt4 in range(4):
                    lt = ltg * 4 + lt4
                    for dc in range(2):
                        b = lt4 * 2 + dc
                        ps, pof = (psA, b) if b < 4 else (psB, b - 4)
                        P.wait("tensor", "V",
                               vsc_done if b in (4, 5) else vs[ltg * 4 + 3])
                        key = (lt4, dc)
                        if key in res_prev:
                            P.wait("tensor", "V", res_prev[key])
                        nmm = 4 + (1 if half == 0 else 0)
                        cnt = 0
                        for pp in range(4):
                            c0 = 3 * pp
                            cnt += 1
                            P.do("tensor", lambda e, c0=c0, lt=lt, pp=pp,
                                 dc=dc, ps=ps, pof=pof, st=(cnt == 1),
                                 sp=(cnt == nmm): e.matmul(
                                ps[:, pof * 512:pof * 512 + 512],
                                xproj[:, c0, ts(lt)],
                                woa[:, pp, ts(dc, 512)],
                                start=st, stop=sp), inc="T")
                        if half == 0:
                            P.do("tensor", lambda e, dc=dc, ps=ps, pof=pof:
                                 e.matmul(
                                ps[:, pof * 512:pof * 512 + 512], ones1[:],
                                boutR[0:1, ts(dc, 512)],
                                start=False, stop=True), inc="T")
                        tm = P.mark("T")
                        P.wait("vector", "T", tm)
                        res_prev[key] = P.do(
                            "vector", lambda e, lt=lt, dc=dc, ps=ps, pof=pof:
                            e.scalar_tensor_tensor(
                                stream[:, lt, ts(dc, 512)],
                                ps[:, pof * 512:pof * 512 + 512], 1.0,
                                stream[:, lt, ts(dc, 512)],
                                op0=Alu.mult, op1=Alu.add), inc="V")
        return P.mark("V")

    def ffn(tag):
        P.wait("sync", "T", P.mark("T"))
        P.wait("sync", "V", P.mark("V"))
        P.wait(G, "T", P.mark("T"))
        P.wait(G, "V", P.mark("V"))
        P.do(G, lambda e: e.dma_start(selE, SelE), inc="dK", by=16)
        P.do(G, lambda e: e.dma_start(b2e, B2E), inc="dK", by=16)
        P.do(G, lambda e: e.dma_start(g_extT[8:9, :], OnesR), inc="dK", by=16)
        feconst = P.mark("dK")
        ones_in = feconst
        ln_tiles_pipe()
        xnT_done = P.mark("V")
        gprev = 0
        P.wait("tensor", "V", xnT_done)
        P.wait("tensor", "dK", ones_in)
        for lt in range(NLT):
            if gprev:
                P.wait("tensor", "V", gprev)
            for k in range(8):
                P.do("tensor", lambda e, lt=lt, k=k: e.matmul(
                    psB[:, 0:8], xnT[:, k, ts(lt)], gatesb[:, k, :],
                    start=(k == 0), stop=(k == 7)), inc="T")
            tg = P.mark("T")
            P.wait("scalar", "T", tg)
            sg = P.do("scalar", lambda e: e.activation(
                gwk[:, 0:8], psB[:, 0:8], AF.Sigmoid), inc="S")
            P.wait("vector", "S", sg)
            seq = [
                lambda e: e.tensor_scalar(gwk[:, 8:16], gwk[:, 0:8],
                                          rs_b[:, 0:1], None, op0=Alu.mult),
                lambda e: e.tensor_reduce(gwk[:, 16:17], gwk[:, 8:16],
                                          axis=mybir.AxisListType.X,
                                          op=Alu.max),
                lambda e: e.tensor_scalar(gwk[:, 0:8], gwk[:, 8:16],
                                          gwk[:, 16:17], None,
                                          op0=Alu.is_equal),
                lambda e: e.scalar_tensor_tensor(gwk[:, 0:8], gwk[:, 0:8],
                                                 -1e38, gwk[:, 8:16],
                                                 op0=Alu.mult, op1=Alu.add),
                lambda e: e.tensor_reduce(gwk[:, 17:18], gwk[:, 0:8],
                                          axis=mybir.AxisListType.X,
                                          op=Alu.max),
                lambda e: e.tensor_tensor(gwk[:, 18:19], gwk[:, 16:17],
                                          gwk[:, 17:18], Alu.add),
                lambda e: e.reciprocal(gwk[:, 19:20], gwk[:, 18:19]),
                lambda e: e.tensor_scalar(gwk[:, 0:8], gwk[:, 8:16],
                                          gwk[:, 17:18], None, op0=Alu.is_ge),
                lambda e: e.scalar_tensor_tensor(gtok[:, 0:8], gwk[:, 8:16],
                                                 gwk[:, 19:20], gwk[:, 0:8],
                                                 op0=Alu.mult, op1=Alu.mult),
            ]
            vg = 0
            for f in seq:
                vg = P.do("vector", f, inc="V")
                P.wait("vector", "V", vg)
            P.wait("tensor", "V", vg)
            tt = P.do("tensor", lambda e: e.transpose(
                psB_bf[0:8, 16:144], gtok[:, 0:8], ident[:]), inc="T")
            P.wait("vector", "T", tt)
            gprev = P.do("vector", lambda e, lt=lt: e.tensor_copy(
                g_extT[0:8, ts(lt)], psB_bf[0:8, 16:144]), inc="V")
        g_all = P.mark("V")

        w1m = {}
        res = {}
        for lc in range(4):
            scprev = {}
            for e_i in range(9):
                buf = e_i % 2
                wsA = "dU0" if buf == 0 else "dU2"
                wsB = "dU1" if buf == 0 else "dU3"
                gi = lc * 9 + e_i
                if gi >= 2:
                    P.wait(G, "T", w1m.get(gi - 2, 0))
                if gi == 0:
                    P.wait(G, "dPC3", 64)
                src = (W1_b[e_i] if e_i < 8 else Ws1_b)
                wmA = P.do(G, lambda e, src=src, buf=buf: e.dma_start(
                    arC.ap()[:, buf * 4096:buf * 4096 + 2048],
                    src[:, 0:2].rearrange("p a b c -> p (a b c)")),
                    inc=wsA, by=16)
                wmB = P.do(G, lambda e, src=src, buf=buf: e.dma_start(
                    arC.ap()[:, buf * 4096 + 2048:(buf + 1) * 4096],
                    src[:, 2:4].rearrange("p a b c -> p (a b c)")),
                    inc=wsB, by=16)
                if lc == 0 and e_i == 0:
                    P.wait("tensor", "V", g_all)
                    P.wait("tensor", "dK", feconst)
                if lc > 0 and e_i == 0:
                    for b in range(8):
                        P.wait("tensor", "V", res[(lc - 1, b)])
                if e_i < 8:
                    if scprev:
                        P.wait("tensor", "V", scprev.get(3, 0))
                    P.do("tensor", lambda e, e_i=e_i, lc=lc: e.matmul(
                        psB[:, 0:512], selE[0:9, ts(e_i)],
                        g_extT[0:9, ts(lc, 512)], start=True, stop=True),
                        inc="T")
                for m in range(4):
                    P.wait("tensor", wsA if m < 2 else wsB,
                           wmA if m < 2 else wmB)
                    if gi >= 1:
                        P.wait("tensor", "S", w1m.get(("gel", m), 0))
                    for k in range(8):
                        P.do("tensor", lambda e, m=m, k=k, buf=buf, lc=lc:
                             e.matmul(psA[:, ts(m, 512)], w1sl(buf, m, k),
                                      xnT[:, k, ts(lc, 512)],
                                      start=(k == 0), stop=(k == 7)), inc="T")
                    tm = P.mark("T")
                    if m == 3:
                        w1m[gi] = tm
                    P.wait("scalar", "T", tm)
                    bias = (b1T, e_i * 4 + m) if e_i < 8 else (bs1T, m)
                    sm = P.do("scalar", lambda e, m=m, e_i=e_i, bias=bias:
                              e.activation(hid[:, e_i, m, :],
                                           psA[:, ts(m, 512)], AF.Gelu,
                                           bias=bias[0][:, bias[1]:
                                                        bias[1] + 1],
                                           scale=1.0), inc="S")
                    w1m[("gel", m)] = sm
                    if e_i < 8:
                        P.wait("vector", "S", sm)
                        scprev[m] = P.do(
                            "vector", lambda e, m=m, e_i=e_i: e.tensor_tensor(
                                hid[:, e_i, m, :], hid[:, e_i, m, :],
                                psB[:, 0:512], Alu.mult), inc="V")
                        w1m[("sc", m)] = scprev[m]
            s_hid = P.mark("S")
            v_hid = P.mark("V")
            w2m = {}
            for e_i in range(9):
                buf = e_i % 2
                wsA = "dSW0" if buf == 0 else "dSW2"
                wsB = "dSW1" if buf == 0 else "dSW3"
                if e_i == 0:
                    P.wait(G, "dPC3", 64)
                if e_i >= 2:
                    P.wait(G, "T", w2m.get(e_i - 2, 0))
                src = (W2_b[e_i] if e_i < 8 else Ws2_b)
                wmA = P.do(G, lambda e, src=src, buf=buf: e.dma_start(
                    arB.ap()[:, buf * 4096:buf * 4096 + 2048],
                    src[:, 0:2].rearrange("p a c -> p (a c)")),
                    inc=wsA, by=16)
                wmB = P.do(G, lambda e, src=src, buf=buf: e.dma_start(
                    arB.ap()[:, buf * 4096 + 2048:(buf + 1) * 4096],
                    src[:, 2:4].rearrange("p a c -> p (a c)")),
                    inc=wsB, by=16)
                if e_i == 0:
                    P.wait("tensor", "S", s_hid)
                    P.wait("tensor", "V", v_hid)
                for hc in range(4):
                    P.wait("tensor", wsA if hc < 2 else wsB,
                           wmA if hc < 2 else wmB)
                    for b in range(8):
                        lt4, dc = b // 2, b % 2
                        ps, pof = (psA, b) if b < 4 else (psB, b - 4)
                        P.do("tensor", lambda e, e_i=e_i, hc=hc, lt4=lt4,
                             dc=dc, ps=ps, pof=pof, buf=buf: e.matmul(
                            ps[:, pof * 512:pof * 512 + 512],
                            hid[:, e_i, hc, ts(lt4)], w2sl(buf, hc, dc),
                            start=(e_i == 0 and hc == 0), stop=False),
                            inc="T")
                w2m[e_i] = P.mark("T")
            for b in range(8):
                lt4, dc = b // 2, b % 2
                ps, pof = (psA, b) if b < 4 else (psB, b - 4)
                P.do("tensor", lambda e, lc=lc, lt4=lt4, dc=dc, ps=ps,
                     pof=pof: e.matmul(
                    ps[:, pof * 512:pof * 512 + 512],
                    g_extT[0:9, lc * 512 + lt4 * 128:
                           lc * 512 + (lt4 + 1) * 128],
                    b2e[0:9, ts(dc, 512)], start=False, stop=True), inc="T")
                tm = P.mark("T")
                P.wait("vector", "T", tm)
                res[(lc, b)] = P.do(
                    "vector", lambda e, lc=lc, lt4=lt4, dc=dc, ps=ps,
                    pof=pof: e.scalar_tensor_tensor(
                        stream[:, lc * 4 + lt4, ts(dc, 512)],
                        ps[:, pof * 512:pof * 512 + 512], 1.0,
                        stream[:, lc * 4 + lt4, ts(dc, 512)],
                        op0=Alu.mult, op1=Alu.add), inc="V")
            if tag == "f2":
                P.wait("sync", "V", P.mark("V"))
                for lt4 in range(4):
                    P.do("sync", lambda e, lc=lc, lt4=lt4: e.dma_start(
                        OUT[ts(lc * 4 + lt4), :], stream[:, lc * 4 + lt4, :]),
                        inc="dO", by=16)
        return P.mark("V")

    def attn_b():
        P.wait("sync", "T", P.mark("T"))
        wo_pc = P.do(G, lambda e: e.dma_start(Wo_b, WoM), inc="dPC", by=16)
        P.wait(G, "T", P.mark("T"))
        ln_tiles_pipe()
        xnT_done = P.mark("V")

        qm = {}
        bqprev = 0
        for o in range(9):
            buf = o % 2
            wsem = "dW0" if buf == 0 else "dW1"
            if o >= 2:
                P.wait(G, "T", qm.get(("mm", o - 2), 0))
            src = Wqt[o] if o < 8 else Wkvt
            wm = P.do(G, lambda e, src=src, buf=buf: e.dma_start(
                wjbuf[:, buf, :, :], src), inc=wsem, by=16)
            P.wait("tensor", wsem, wm)
            if o == 0:
                P.wait("tensor", "V", xnT_done)
            if bqprev:
                P.wait("tensor", "V", bqprev)
            for k in range(8):
                P.do("tensor", lambda e, k=k, buf=buf: e.matmul(
                    psB[:, 0:1], wjbuf[:, buf, k, :], lnbTb[:, k:k + 1],
                    start=(k == 0), stop=(k == 7)), inc="T")
            tbq = P.mark("T")
            P.wait("vector", "T", tbq)
            dstb = invq[:, o:o + 1] if o < 8 else bkv[:, 0:1]
            bqprev = P.do("vector", lambda e, dstb=dstb: e.tensor_copy(
                dstb, psB[:, 0:1]), inc="V")
            vscl = 0
            for k in range(8):
                vscl = P.do("vector", lambda e, k=k, buf=buf: e.tensor_scalar(
                    wjbuf[:, buf, k, :], wjbuf[:, buf, k, :],
                    lngT[:, k:k + 1], None, op0=Alu.mult), inc="V")
            P.wait("tensor", "V", vscl)
            for lsub in range(4):
                for k in range(8):
                    P.do("tensor", lambda e, k=k, lsub=lsub, buf=buf:
                         e.matmul(psA[:, ts(lsub, 512)], wjbuf[:, buf, k, :],
                                  xnT[:, k, ts(lsub, 512)],
                                  start=(k == 0), stop=(k == 7)), inc="T")
            qm[("mm", o)] = P.mark("T")
            P.wait("vector", "T", qm[("mm", o)])
            for lsub in range(4):
                if o < 8:
                    qm[("ev", o)] = P.do(
                        "vector", lambda e, o=o, lsub=lsub: e.tensor_scalar(
                            qT[:, o, ts(lsub, 512)], psA[:, ts(lsub, 512)],
                            invq[:, o:o + 1], 0.125,
                            op0=Alu.add, op1=Alu.mult), inc="V")
                else:
                    qm[("ev", o)] = P.do(
                        "vector", lambda e, lsub=lsub: e.tensor_scalar(
                            kvb[:, ts(lsub, 512)], psA[:, ts(lsub, 512)],
                            bkv[:, 0:1], None, op0=Alu.add), inc="V")
        kv_done = qm[("ev", 8)]

        P.wait(G, "V", kv_done)
        kd = P.do(G, lambda e: e.dma_start(
            kdup[64:128, :], kvb[0:64, :]), inc="dK", by=16)
        P.wait("tensor", "V", kv_done)
        P.do("vector", lambda e: e.memset(vtok[:, :, DH:DH + 1], 1.0))
        for mt in range(16):
            if mt == 8:
                P.wait("tensor", "V", P.mark("V"))
            P.do("tensor", lambda e, mt=mt: e.transpose(
                psB_bf[:, 2048 + (mt % 8) * 64:2048 + (mt % 8) * 64 + 64],
                kvb[64:128, ts(mt)], ident[64:128, 64:128]), inc="T")
            if mt % 8 == 7:
                tm = P.mark("T")
                P.wait("vector", "T", tm)
                P.do("vector", lambda e, mt=mt: e.tensor_copy(
                    vtok[:, mt - 7:mt + 1, 0:DH],
                    psB_bf[:, 2048:2560].rearrange("p (a c) -> p a c", a=8)),
                    inc="V")
        vtok_done = P.do("vector", lambda e: e.nop(), inc="V")
        P.wait("scalar", "dT", tbl_done)
        P.wait(G, "dK", P.mark("dK"))
        P.wait("tensor", "dK", P.mark("dK"))

        rb_dummy = 0

        def swv2(s, par, n):
            off = SWBASE + s * 2048 + par * 1024
            return arC.ap()[:, off:off + n]

        def etv2(s, par, a, b):
            off = ETBASE + s * 2048 + par * 1024
            return arC.ap()[:, off + a:off + b]

        def hstream(s):
            soff = s * 1024
            sgsem = ("dSG0", "dSG1") if s == 0 else ("dSG2", "dSG3")
            enl = enull_p[s]
            vnl = vnule[64:65, :]
            ksrc = kvb if s == 0 else kdup
            base = s * 64
            tasks = [(h, qh) for h in range(s, 16, 2) for qh in (0, 1)]
            swm_p = {}
            sw_marks = {}
            rbq = {}
            rb_done_t = {}
            et_free = {}
            exp_d = {}
            odd_dma = 0

            def nsteps(ti):
                return 8 * (tasks[ti][1] + 1)

            def sw_issue(ti, m):
                h, qh = tasks[ti]
                qc0l = max(0, m // 4 - 2 * qh)
                cntl = 2 - qc0l
                par = m % 2
                key = (ti, m - 2)
                if key not in swm_p and m < 2 and ti > 0:
                    key = (ti - 1, nsteps(ti - 1) - 2 + m)
                if key in swm_p:
                    P.wait(G, "V", swm_p[key])
                off = h * 4095 + 128 * m + 1024 * (1 - qh)
                mk = P.do(G, lambda e, off=off, n=512 * cntl, par=par:
                          e.dma_start(swv2(s, par, n),
                                      bass.AP(TBLh, off, [[1, 128], [1, n]])),
                          inc=sgsem[par], by=16)
                sw_marks[(ti, m)] = (sgsem[par], mk)

            def nulls_for(ti):
                h, qh = tasks[ti]
                for l in range(2):
                    P.do("tensor", lambda e, h=h, qh=qh, l=l: e.matmul(
                        psA[64:65, soff + l * 512:soff + l * 512 + 512],
                        nulkd[base:base + 64, 0:1],
                        qT[base:base + 64, h // 2,
                           qh * 1024 + l * 512:qh * 1024 + l * 512 + 512],
                        start=True, stop=True), inc="T")
                tnn = P.mark("T")
                P.wait("scalar", "T", tnn)
                P.wait("scalar", "S", P.mark("S"))
                return P.do("scalar", lambda e, h=h, qh=qh: e.activation(
                    enl[:, qh * 1024:qh * 1024 + 1024],
                    psA[64:65, soff:soff + 1024], AF.Exp,
                    bias=nabm8[0:1, h:h + 1]), inc="S")

            P.wait("tensor", "V", vtok_done)
            snl = nulls_for(0)
            sw_issue(0, 0)
            sw_issue(0, 1)
            yield
            for ti, (h, qh) in enumerate(tasks):
                mh = nsteps(ti)
                exp_prev = snl
                exp_pp = snl
                stt_prev = P.mark("V")

                def emit_av(m, ti=ti, h=h, qh=qh):
                    qc0m = max(0, m // 4 - 2 * qh)
                    par = m % 2
                    P.wait("tensor", "S", exp_d[(ti, m)])
                    if m == 0 and ti > 0:
                        P.wait("tensor", "V", rb_done_t[ti - 1])
                    lastm = 0
                    for l in range(qc0m, 2):
                        lastm = P.do("tensor", lambda e, m=m, l=l, par=par:
                                     e.matmul(
                            psB[0:65, soff + l * 512:soff + l * 512 + 512],
                            vtok[:, m, :],
                            etv2(s, par, l * 512, l * 512 + 512),
                            start=(m == 0), stop=False), inc="T")
                    et_free[(ti, m)] = lastm

                for mt in range(mh):
                    qc0l = max(0, mt // 4 - 2 * qh)
                    cntl = 2 - qc0l
                    par = mt % 2
                    swsem, swm = sw_marks[(ti, mt)]
                    if mt == 0:
                        P.wait("tensor", "S", snl)
                    else:
                        P.wait("tensor", "V", stt_prev)
                    for l in range(qc0l, 2):
                        if mt == 0 and ti > 0:
                            P.wait("tensor", "V", rbq[(ti - 1, l)])
                        P.do("tensor", lambda e, l=l, mt=mt, h=h, qh=qh:
                             e.matmul(
                            psA[:, soff + l * 512:soff + l * 512 + 512],
                            ksrc[base:base + 64, ts(mt)],
                            qT[base:base + 64, h // 2,
                               qh * 1024 + l * 512:qh * 1024 + l * 512 + 512],
                            start=True, stop=True), inc="T")
                    tsc = P.mark("T")
                    P.wait("vector", "T", tsc)
                    P.wait("vector", swsem, swm)
                    bbuf = biased if par == 0 else biased2v
                    P.wait("vector", "S", exp_pp)
                    if mt <= 1 and ti > 0:
                        P.wait("vector", "V", rb_done_t[ti - 1])
                    w = 512 * cntl
                    swrev = bass.AP(arC, SWBASE + s * 2048 + par * 1024
                                    + w - 1, [[10240, 128], [-1, w]])
                    vbias = P.do("vector", lambda e, qc0l=qc0l, w=w,
                                 swrev=swrev, bbuf=bbuf:
                                 e.scalar_tensor_tensor(
                        bbuf[:, soff + qc0l * 512:soff + 1024],
                        psA[:, soff + qc0l * 512:soff + 1024],
                        -CSTAB, swrev, op0=Alu.add, op1=Alu.add), inc="V")
                    swm_p[(ti, mt)] = vbias
                    stt_prev = vbias
                    if mt + 2 < mh:
                        sw_issue(ti, mt + 2)
                    elif ti + 1 < len(tasks):
                        sw_issue(ti + 1, mt + 2 - mh)
                    P.wait("scalar", "V", vbias)
                    if (ti, mt - 2) in et_free:
                        P.wait("scalar", "T", et_free[(ti, mt - 2)])
                    exp_pp = exp_prev
                    exp_prev = P.do("scalar", lambda e, qc0l=qc0l, par=par,
                                    bbuf=bbuf: e.activation(
                        etv2(s, par, qc0l * 512, 1024),
                        bbuf[:, soff + qc0l * 512:soff + 1024], AF.Exp),
                        inc="S")
                    exp_d[(ti, mt)] = exp_prev
                    if mt > 0:
                        emit_av(mt - 1)
                    if mt == mh - 1:
                        emit_av(mt)
                    if mt == 8 * qh + 4:
                        P.do("tensor", lambda e, qh=qh: e.matmul(
                            psB[0:65, soff:soff + 512], vnl,
                            enl[:, qh * 1024:qh * 1024 + 512],
                            start=False, stop=True), inc="T")
                        tq0 = P.mark("T")
                        P.wait("scalar", "T", tq0)
                        P.wait("scalar", "S", P.mark("S"))
                        P.do("scalar", lambda e: e.activation(
                            invw[64:65, soff:soff + 512],
                            psB[64:65, soff:soff + 512], AF.Ln), inc="S")
                        P.wait("scalar", "S", P.mark("S"))
                        exq0 = P.do("scalar", lambda e: e.activation(
                            invb[64:65, soff:soff + 512],
                            invw[64:65, soff:soff + 512], AF.Exp,
                            scale=-1.0), inc="S")
                    if mt == 8 * qh + 5:
                        P.wait("tensor", "S", exq0)
                        P.do("tensor", lambda e: e.matmul(
                            psA[0:64, soff:soff + 512], onesT[64:65, 0:64],
                            invb[64:65, soff:soff + 512],
                            start=True, stop=True), inc="T")
                        trb0 = P.mark("T")
                        P.wait("vector", "T", trb0)
                        P.wait("vector", "S", exp_d[(ti, mt - 2)])
                        if s == 1 and odd_dma:
                            P.wait("vector", "dOD", odd_dma)
                        rbq[(ti, 0)] = P.do("vector", lambda e: e.tensor_copy(
                            biased[0:64, soff:soff + 512],
                            psA[0:64, soff:soff + 512]), inc="V")
                        P.wait("vector", "V", rbq[(ti, 0)])
                        col0 = qh * 1024
                        dst0 = (optT[0:64, h // 2, col0:col0 + 512] if s == 0
                                else oddsl[0:64, col0:col0 + 512])
                        ml0 = P.do("vector", lambda e, dst0=dst0:
                                   e.tensor_tensor(
                            dst0, psB[0:64, soff:soff + 512],
                            biased[0:64, soff:soff + 512], Alu.mult),
                            inc="V")
                    yield
                s_lastexp = P.mark("S")
                P.wait("tensor", "V", stt_prev)
                P.wait("tensor", "S", s_lastexp)
                P.do("tensor", lambda e, qh=qh: e.matmul(
                    psB[0:65, soff + 512:soff + 1024],
                    vnl, enl[:, qh * 1024 + 512:qh * 1024 + 1024],
                    start=False, stop=True), inc="T")
                tav = P.mark("T")
                if ti + 1 < len(tasks):
                    snl = nulls_for(ti + 1)
                yield
                P.wait("scalar", "T", tav)
                P.wait("scalar", "S", P.mark("S"))
                P.do("scalar", lambda e: e.activation(
                    invw[64:65, soff + 512:soff + 1024],
                    psB[64:65, soff + 512:soff + 1024], AF.Ln), inc="S")
                P.wait("scalar", "S", P.mark("S"))
                exq1 = P.do("scalar", lambda e: e.activation(
                    invb[64:65, soff + 512:soff + 1024],
                    invw[64:65, soff + 512:soff + 1024],
                    AF.Exp, scale=-1.0), inc="S")
                P.wait("vector", "S", s_lastexp)
                P.wait("tensor", "S", exq1)
                P.do("tensor", lambda e: e.matmul(
                    psA[0:64, soff + 512:soff + 1024],
                    onesT[64:65, 0:64],
                    invb[64:65, soff + 512:soff + 1024],
                    start=True, stop=True), inc="T")
                trb = P.mark("T")
                P.wait("vector", "T", trb)
                rbq[(ti, 1)] = P.do("vector", lambda e: e.tensor_copy(
                    biased[0:64, soff + 512:soff + 1024],
                    psA[0:64, soff + 512:soff + 1024]), inc="V")
                P.wait("vector", "V", rbq[(ti, 1)])
                col = qh * 1024 + 512
                dstv = (optT[0:64, h // 2, col:col + 512] if s == 0
                        else oddsl[0:64, col:col + 512])
                rb_done_t[ti] = P.do(
                    "vector", lambda e, dstv=dstv: e.tensor_tensor(
                        dstv,
                        psB[0:64, soff + 512:soff + 1024],
                        biased[0:64, soff + 512:soff + 1024],
                        Alu.mult), inc="V")
                if s == 1 and qh == 1:
                    P.wait(G, "V", rb_done_t[ti])
                    odd_dma = P.do(G, lambda e, h=h: e.dma_start(
                        optT[64:128, h // 2, :], oddsl[0:64, :]),
                        inc="dOD", by=16)
                yield

        g0 = hstream(0)
        g1 = hstream(1)
        alive = [True, True]
        gens = [g0, g1]
        while alive[0] or alive[1]:
            for gi in (0, 1):
                if alive[gi]:
                    try:
                        next(gens[gi])
                    except StopIteration:
                        alive[gi] = False
        heads_done = P.mark("V")
        odd_in = P.mark("dOD")

        outg_b = cb(arB, 0, [1024])
        outb_b = cb(arB, 1024, [1024])
        olg = cf(arB, 2048, [1024])[0:1, :]
        olb = cf(arB, 6144, [1024])[0:1, :]
        wores = cb(arC, 2048, [8, 1024])
        P.wait("sync", "V", heads_done)
        P.wait("sync", "dPC", wo_pc)
        P.do("sync", lambda e: e.dma_start(olg, OutLnG), inc="dWo", by=16)
        rowm = P.do("sync", lambda e: e.dma_start(olb, OutLnB), inc="dWo",
                    by=16)
        for i in range(8):
            P.do("sync", lambda e, i=i: e.dma_start(
                wores[:, i, :], Wo_b[i]), inc="dWo", by=16)
        wo_in = P.mark("dWo")
        P.wait("tensor", "dWo", wo_in)
        P.wait("tensor", "V", heads_done)
        for c in range(2):
            P.do("tensor", lambda e, c=c: e.matmul(
                psA[:, ts(c, 512)], ones1f[:], olg[0:1, ts(c, 512)],
                start=True, stop=True), inc="T")
            P.do("tensor", lambda e, c=c: e.matmul(
                psA[:, 1024 + c * 512:1536 + c * 512], ones1f[:],
                olb[0:1, ts(c, 512)], start=True, stop=True), inc="T")
        tgb = P.mark("T")
        P.wait("vector", "T", tgb)
        for c in range(2):
            P.do("vector", lambda e, c=c: e.tensor_copy(
                outg_b[:, ts(c, 512)], psA[:, ts(c, 512)]), inc="V")
            P.do("vector", lambda e, c=c: e.tensor_copy(
                outb_b[:, ts(c, 512)], psA[:, 1024 + c * 512:1536 + c * 512]),
                inc="V")
        gb_done = P.mark("V")

        ywk = [ywork, cf(arD, 2048, [1024]), cf(arD, 4096, [1024])]
        resm = {}
        lnm = {}
        lnv5 = 0
        P.wait("tensor", "dWo", wo_in)
        P.wait("tensor", "V", gb_done)
        P.wait("tensor", "dOD", odd_in)
        for sub in range(8):
            half = sub % 2
            bs = 0 if half == 0 else 4
            pset = psA if half == 0 else psB
            for i in range(8):
                for j in range(2):
                    lt = sub * 2 + j
                    for dc in range(2):
                        b = j * 2 + dc
                        if i == 0 and sub >= 2:
                            P.wait("tensor", "V", resm[(sub - 2, j, dc)])
                        P.do("tensor", lambda e, i=i, lt=lt, dc=dc, b=b,
                             pset=pset: e.matmul(
                            pset[:, b * 512:b * 512 + 512],
                            optT[:, i, ts(lt)],
                            wores[:, i, ts(dc, 512)],
                            start=(i == 0), stop=(i == 7)), inc="T")
            twq = P.mark("T")
            P.wait("vector", "T", twq)
            for j in range(2):
                lt = sub * 2 + j
                yw = ywk[lt % 3]
                if lt >= 3:
                    P.wait("vector", "V", resm[((lt - 3) // 2, (lt - 3) % 2,
                                               1)])
                    P.wait("vector", "Gs", lnm.get(lt - 3, 0))
                P.wait("scalar", "T", twq)
                P.wait("scalar", "S", P.mark("S"))
                P.wait("scalar", "V", lnv5)
                P.do("scalar", lambda e, pset=pset, j=j: e.activation(
                    biased[:, 0:1024], pset[:, j * 1024:j * 1024 + 1024],
                    AF.Identity, scale=1.0 / 1024.0,
                    accum_out=lnscr[:, 8:9]), inc="S")
                ssq = P.do("scalar", lambda e, pset=pset, j=j: e.activation(
                    biased[:, 1024:2048], pset[:, j * 1024:j * 1024 + 1024],
                    AF.Square, accum_out=lnscr[:, 9:10]), inc="S")
                P.wait("vector", "S", ssq)
                P.do("vector", lambda e: e.tensor_scalar(
                    lnscr[:, 10:11], lnscr[:, 9:10], 1.0 / 1024.0, None,
                    op0=Alu.mult))
                v1 = P.do("vector", lambda e: e.scalar_tensor_tensor(
                    lnscr[:, 11:12], lnscr[:, 8:9], 1.0, lnscr[:, 8:9],
                    op0=Alu.mult, op1=Alu.mult), inc="V")
                P.wait("vector", "V", v1)
                v2 = P.do("vector", lambda e: e.tensor_tensor(
                    lnscr[:, 9:10], lnscr[:, 10:11], lnscr[:, 11:12],
                    Alu.subtract), inc="V")
                P.wait("scalar", "V", v2)
                s0 = P.do("scalar", lambda e: e.activation(
                    lnscr[:, 10:11], lnscr[:, 9:10], AF.Ln, bias=epsb[:],
                    scale=1.0), inc="S")
                P.wait("scalar", "S", s0)
                s1 = P.do("scalar", lambda e: e.activation(
                    lnscr[:, 11:12], lnscr[:, 10:11], AF.Exp, scale=-0.5),
                    inc="S")
                P.wait("vector", "S", s1)
                for dc in range(2):
                    b = j * 2 + dc
                    v5 = P.do("vector", lambda e, pset=pset, b=b, dc=dc,
                              yw=yw: e.tensor_scalar(
                        yw[:, dc * 512:dc * 512 + 512],
                        pset[:, b * 512:b * 512 + 512],
                        lnscr[:, 8:9], lnscr[:, 11:12],
                        op0=Alu.subtract, op1=Alu.mult), inc="V")
                lnv5 = v5
                P.wait("vector", "V", v5)
                P.wait(G, "V", v5)
                v6 = P.do("vector", lambda e, yw=yw: e.tensor_tensor(
                    yw[:, 0:512], yw[:, 0:512], outg_b[:, 0:512], Alu.mult),
                    inc="V")
                g6 = P.do(G, lambda e, yw=yw: e.tensor_tensor(
                    yw[:, 512:1024], yw[:, 512:1024], outg_b[:, 512:1024],
                    Alu.mult), inc="Gs")
                P.wait("vector", "V", v6)
                P.wait(G, "Gs", g6)
                v7 = P.do("vector", lambda e, yw=yw: e.tensor_tensor(
                    yw[:, 0:512], yw[:, 0:512], outb_b[:, 0:512], Alu.add),
                    inc="V")
                g7 = P.do(G, lambda e, yw=yw: e.tensor_tensor(
                    yw[:, 512:1024], yw[:, 512:1024], outb_b[:, 512:1024],
                    Alu.add), inc="Gs")
                P.wait("vector", "V", v7)
                resm[(sub, j, 0)] = P.do(
                    "vector", lambda e, lt=lt, yw=yw: e.tensor_tensor(
                        stream[:, lt, 0:512], stream[:, lt, 0:512],
                        yw[:, 0:512], Alu.add), inc="V")
                P.wait("vector", "Gs", g7)
                resm[(sub, j, 1)] = P.do(
                    "vector", lambda e, lt=lt, yw=yw: e.tensor_tensor(
                        stream[:, lt, 512:1024], stream[:, lt, 512:1024],
                        yw[:, 512:1024], Alu.add), inc="V")
                lnm[lt] = g7
        return P.mark("V")

    last = attn_a()
    if _PHASES >= 2:
        last = ffn("f1")
    if _PHASES >= 3:
        last = attn_b()
    if _PHASES >= 4:
        last = ffn("f2")

    P.wait("sync", "V", last)
    for s_ in ["T", "V", "S", "Gs", "dK", "dSG0", "dSG1", "dOD", "dW0",
               "dW1", "dU0", "dU1", "dT", "dPC", "dPC2", "dPC3", "dWo",
               "dR", "dP1", "dX", "dC", "dSW0", "dSW1", "dSG2", "dSG3",
               "dU2", "dU3", "dSW2", "dSW3"]:
        P.wait("sync", s_, P.mark(s_))
    if _PHASES < 4:
        for lt in range(NLT):
            P.do("sync", lambda e, lt=lt: e.dma_start(
                OUT[ts(lt), :], stream[:, lt, :]), inc="dO", by=16)
    P.wait("sync", "dO", P.mark("dO"))

    with nc.Block() as block:
        for eng in ENGS:
            def section(e, eng=eng):
                for f in P.ops[eng]:
                    f(e)
            getattr(block, eng)(section)

    stack.close()
    return nc


def _host_prep(inputs):
    f = np.float32
    gi = lambda k: np.asarray(inputs[k], f)
    w_in, w_out = gi("L0_attn_a_w_in"), gi("L0_attn_a_w_out")
    b_in, b_out = gi("L0_attn_a_b_in"), gi("L0_attn_a_b_out")
    Wq, Wkv, Wo = gi("L0_attn_b_Wq"), gi("L0_attn_b_Wkv"), gi("L0_attn_b_Wo")
    gate_W = gi("L0_ffn_gate_W")
    W1, W2 = gi("L0_ffn_W1"), gi("L0_ffn_W2")
    Ws1, Ws2 = gi("L0_ffn_Ws1"), gi("L0_ffn_Ws2")
    b1, b2 = gi("L0_ffn_b1"), gi("L0_ffn_b2")
    bs1, bs2 = gi("L0_ffn_bs1"), gi("L0_ffn_bs2")

    d = {}
    w_inT = np.ascontiguousarray(w_in.T)
    d["w_inT_t"] = np.ascontiguousarray(
        w_inT.reshape(8, 128, 24, 128).transpose(2, 1, 0, 3))
    d["b_inT"] = np.ascontiguousarray(b_in.reshape(24, 128).T)
    d["w_outT_t"] = np.ascontiguousarray(w_out.T.reshape(8, 128, 1024))
    d["b_out_row"] = b_out.reshape(1, D)
    d["gate_W_t"] = np.ascontiguousarray(
        gate_W.reshape(8, 128, 8).transpose(1, 0, 2))
    d["W1_t"] = np.ascontiguousarray(
        W1.reshape(8, 8, 128, 4, 128).transpose(0, 2, 3, 1, 4))
    d["b1T"] = np.ascontiguousarray(
        b1.reshape(8, 4, 128).transpose(2, 0, 1).reshape(128, 32))
    d["W2_t"] = np.ascontiguousarray(
        W2.reshape(8, 4, 128, 1024).transpose(0, 2, 1, 3))
    d["b2ext"] = np.ascontiguousarray(
        np.concatenate([b2, bs2.reshape(1, D)], axis=0))
    d["Ws1_t"] = np.ascontiguousarray(
        Ws1.reshape(8, 128, 4, 128).transpose(1, 2, 0, 3))
    d["bs1T"] = np.ascontiguousarray(bs1.reshape(4, 128).T)
    d["Ws2_t"] = np.ascontiguousarray(
        Ws2.reshape(4, 128, 1024).transpose(1, 0, 2))
    d["route_scale"] = gi("L0_ffn_route_scale").reshape(1, 1)
    d["ones_row"] = np.ones((1, L), f)
    d["Wq_t"] = np.ascontiguousarray(
        Wq.reshape(8, 128, 8, 128).transpose(2, 1, 0, 3))
    d["Wkv_t"] = np.ascontiguousarray(
        Wkv.reshape(8, 128, 128).transpose(1, 0, 2))
    d["lngT"] = np.ascontiguousarray(gi("L0_attn_b_ln_g").reshape(8, 128).T)
    d["lnbT"] = np.ascontiguousarray(gi("L0_attn_b_ln_b").reshape(8, 128).T)
    d["Wo_m"] = np.ascontiguousarray(Wo.reshape(8, 128, 1024))
    d["out_ln_g"] = gi("L0_attn_b_out_ln_g").reshape(1, D)
    d["out_ln_b"] = gi("L0_attn_b_out_ln_b").reshape(1, D)
    nk = gi("L0_attn_b_null_kv")
    d["null_k_dup"] = np.ascontiguousarray(
        np.concatenate([nk[0], nk[0]]).reshape(128, 1))
    d["vnull_ext"] = np.ascontiguousarray(
        np.concatenate([nk[1], np.ones(1, f)]).reshape(1, DH + 1))
    d["nab_row"] = gi("L0_attn_b_null_attn_bias").reshape(1, NHEAD)
    d["pb_w0"] = gi("L0_attn_b_pb_w0").reshape(1, D)
    d["pb_w1_t"] = np.ascontiguousarray(
        gi("L0_attn_b_pb_w1").reshape(8, 128, 8, 128).transpose(1, 2, 0, 3))
    d["pb_w2_t"] = np.ascontiguousarray(
        gi("L0_attn_b_pb_w2").reshape(8, 128, 16).transpose(1, 0, 2))
    for nm, key in [("pb_b0T", "L0_attn_b_pb_b0"),
                    ("pb_ln0_gT", "L0_attn_b_pb_ln0_g"),
                    ("pb_ln0_bT", "L0_attn_b_pb_ln0_b"),
                    ("pb_b1T", "L0_attn_b_pb_b1"),
                    ("pb_ln1_gT", "L0_attn_b_pb_ln1_g"),
                    ("pb_ln1_bT", "L0_attn_b_pb_ln1_b")]:
        d[nm] = np.ascontiguousarray(gi(key).reshape(8, 128).T)
    d["pb_b2c"] = gi("L0_attn_b_pb_b2").reshape(16, 1)
    d["pb_b0_row"] = gi("L0_attn_b_pb_b0").reshape(1, D)
    d["ident"] = np.eye(128, dtype=f)
    ps = np.zeros((8, 4, 128), f)
    for pp in range(4):
        ps[2 * pp, pp, 0:64] = 1.0
        ps[2 * pp + 1, pp, 64:128] = 1.0
    d["pairsel"] = np.ascontiguousarray(ps.reshape(8, 512))
    se = np.zeros((9, 8, 128), f)
    for e_i in range(8):
        se[e_i, e_i, :] = 1.0
    d["selE"] = np.ascontiguousarray(se.reshape(9, 1024))
    return d


def kernel(**inputs):
    shared = _host_prep(inputs)
    x = np.asarray(inputs["L0_attn_a_x"], np.float32)
    nc = build_nc()
    in_maps = []
    for core in range(NCORES):
        m = dict(shared)
        m["x"] = np.ascontiguousarray(x[core])
        in_maps.append(m)
    res = run_bass_kernel_spmd(nc, in_maps, core_ids=list(range(NCORES)))
    return np.stack([res.results[c]["out"] for c in range(NCORES)], axis=0)


# version 47
# speedup vs baseline: 1.0115x; 1.0115x over previous
import contextlib
import numpy as np
import ml_dtypes
import concourse.bass as bass
import concourse.mybir as mybir
from concourse.bass_utils import run_bass_kernel_spmd

F32 = mybir.dt.float32
BF16 = mybir.dt.bfloat16
AF = mybir.ActivationFunctionType
Alu = mybir.AluOpType

NCORES = 8
L = 2048
D = 1024
NLT = 16
NHEAD = 16
DH = 64
CSTAB = 8.0
_PHASES = 4
_SUB = 99

ENGS = ("tensor", "vector", "scalar", "gpsimd", "sync")
G = "gpsimd"


def ts(i, n=128):
    return slice(i * n, (i + 1) * n)


class Prog:
    def __init__(self, sems):
        self.ops = {e: [] for e in ENGS}
        self.cnt = {name: 0 for name in sems}
        self.sems = sems

    def do(self, eng, fn, inc=None, by=1):
        if inc is not None:
            self.cnt[inc] += by
            sem = self.sems[inc]

            def run(e, fn=fn, sem=sem, n=by):
                fn(e).then_inc(sem, n)
            self.ops[eng].append(run)
            return self.cnt[inc]
        self.ops[eng].append(lambda e, fn=fn: fn(e))
        return None

    def wait(self, eng, sem, val=None):
        v = self.cnt[sem] if val is None else val
        if v and v > 0:
            s = self.sems[sem]
            self.ops[eng].append(lambda e, s=s, v=v: e.wait_ge(s, v))

    def mark(self, sem):
        return self.cnt[sem]


def build_nc():
    nc = bass.Bass("TRN2", target_bir_lowering=False)

    def din(name, shape, dtype=F32):
        return nc.dram_tensor(name, shape, dtype, kind="ExternalInput").ap()

    X = din("x", [L, D])
    WinT = din("w_inT_t", [24, 128, 8, 128])
    BinT = din("b_inT", [128, 24])
    WoutT = din("w_outT_t", [8, 128, 1024])
    BoutR = din("b_out_row", [1, D])
    GateW = din("gate_W_t", [128, 8, 8])
    W1t = din("W1_t", [8, 128, 4, 8, 128])
    B1T = din("b1T", [128, 32])
    W2t = din("W2_t", [8, 128, 4, 1024])
    B2E = din("b2ext", [9, D])
    Ws1t = din("Ws1_t", [128, 4, 8, 128])
    Bs1T = din("bs1T", [128, 4])
    Ws2t = din("Ws2_t", [128, 4, 1024])
    RS = din("route_scale", [1, 1])
    OnesR = din("ones_row", [1, L])
    Wqt = din("Wq_t", [8, 128, 8, 128])
    Wkvt = din("Wkv_t", [128, 8, 128])
    LnG = din("lngT", [128, 8])
    LnB = din("lnbT", [128, 8])
    WoM = din("Wo_m", [8, 128, 1024])
    OutLnG = din("out_ln_g", [1, D])
    OutLnB = din("out_ln_b", [1, D])
    NulKD = din("null_k_dup", [128, 1])
    VNulE = din("vnull_ext", [1, DH + 1])
    NabR = din("nab_row", [1, NHEAD])
    PbW0 = din("pb_w0", [1, D])
    PbW1t = din("pb_w1_t", [128, 8, 8, 128])
    PbW2t = din("pb_w2_t", [128, 8, 16])
    PbB0 = din("pb_b0T", [128, 8])
    PbL0G = din("pb_ln0_gT", [128, 8])
    PbL0B = din("pb_ln0_bT", [128, 8])
    PbB1 = din("pb_b1T", [128, 8])
    PbL1G = din("pb_ln1_gT", [128, 8])
    PbL1B = din("pb_ln1_bT", [128, 8])
    PbB2 = din("pb_b2c", [16, 1])
    B0R = din("pb_b0_row", [1, D])
    Ident = din("ident", [128, 128])
    PairSel = din("pairsel", [8, 512])
    SelE = din("selE", [9, 1024])

    OUT = nc.dram_tensor("out", [L, D], F32, kind="ExternalOutput").ap()

    def dscr(name, shape, dtype=BF16):
        return nc.dram_tensor(name, shape, dtype, kind="Internal")

    WinT_b = dscr("winT_b", [24, 128, 8, 128]).ap()
    WoutT_b = dscr("woutT_b", [8, 128, 1024]).ap()
    W1_b = dscr("w1_b", [8, 128, 4, 8, 128]).ap()
    W2_b = dscr("w2_b", [8, 128, 4, 1024]).ap()
    Ws1_b = dscr("ws1_b", [128, 4, 8, 128]).ap()
    Ws2_b = dscr("ws2_b", [128, 4, 1024]).ap()
    Wq_b = dscr("wq_b", [8, 128, 8, 128]).ap()
    Wkv_b = dscr("wkv_b", [128, 8, 128]).ap()
    Wo_b = dscr("wo_b", [8, 128, 1024]).ap()
    PbW1_b = dscr("pbw1_b", [128, 8, 8, 128]).ap()
    TBLh = dscr("tbl", [NHEAD, 4095])
    TBL = TBLh.ap()

    semnames = ["dX", "dC", "dPC", "dPC2", "dPC3", "dWo", "dSG2", "dSG3",
                "dU2", "dU3", "dSW2", "dSW3", "dT", "dO", "dK", "dR", "dP1", "dSG0", "dSG1",
                "dW0", "dW1", "dU0", "dU1", "dSW0", "dSW1", "dOD",
                "T", "V", "S", "Gs"]
    stack = contextlib.ExitStack()
    sems = {n: stack.enter_context(nc.semaphore(n)) for n in semnames}
    P = Prog(sems)

    def sbuf(name, shape, dtype):
        return stack.enter_context(nc.sbuf_tensor(name, shape, dtype))

    stream = sbuf("stream", [128, NLT, 1024], F32)
    xnT = sbuf("xnT", [128, 8, 2048], BF16)
    arA = sbuf("arA", [128, 24576], BF16)
    arB = sbuf("arB", [128, 8192], BF16)
    arC = sbuf("arC", [128, 10240], BF16)
    arD = sbuf("arD", [128, 6144], BF16)
    xn_tok = sbuf("xn_tok", [128, 2, 1024], BF16)
    ident = sbuf("identb", [128, 128], BF16)
    ident32f = sbuf("ident32f", [128, 128], F32)
    ones1 = sbuf("ones1", [1, 128], BF16)
    ones1f = sbuf("ones1f", [1, 128], F32)
    onesc = sbuf("onesc", [128, 1], BF16)
    onesT = sbuf("onesT", [128, 128], BF16)
    cm8 = sbuf("cm8", [128, 1], F32)
    epsb = sbuf("epsb", [128, 1], F32)
    lnscr = sbuf("lnscr", [128, 12], F32)
    bnst = sbuf("bnst", [128, 4, 6], F32)
    binT = sbuf("binT_s", [128, 24], F32)
    gatesb = sbuf("gatesb", [128, 8, 8], BF16)
    b1T = sbuf("b1T_s", [128, 32], F32)
    bs1T = sbuf("bs1T_s", [128, 4], F32)
    rsb = sbuf("rsb", [1, 1], F32)
    rs_b = sbuf("rs_b", [128, 1], F32)
    lngT = sbuf("lngT_s", [128, 8], F32)
    lnbTb = sbuf("lnbT_s", [128, 8], BF16)
    nabr = sbuf("nabr_s", [1, NHEAD], F32)
    nabm8 = sbuf("nabm8", [1, NHEAD], F32)
    nulkd = sbuf("nulkd_s", [128, 1], BF16)
    vnule = sbuf("vnule_s", [65, DH + 1], BF16)
    pbw2 = sbuf("pbw2_s", [128, 8, 16], BF16)
    pbc = sbuf("pbc", [128, 6, 8], F32)
    pbb2c = sbuf("pbb2c", [16, 1], F32)
    p0sc = sbuf("p0sc", [1, 8], F32)
    Sa = sbuf("Sa", [128, 128], F32)
    Ra = sbuf("Ra", [128, 128], F32)
    invq = sbuf("invq", [128, 8], F32)
    bkv = sbuf("bkv", [128, 1], F32)
    gwk = sbuf("gwk", [128, 24], F32)
    gtok = sbuf("gtok", [128, 8], BF16)

    psA_h = stack.enter_context(nc.psum_tensor("psA", [128, 2048], F32))
    psB_h = stack.enter_context(nc.psum_tensor("psB", [128, 2048], F32))
    psA = psA_h.ap()
    psB = psB_h.ap()
    psB_bf = psB_h.ap().bitcast(BF16)

    def cb(ar, off, shape):
        v = ar.ap()[:, off:off + int(np.prod(shape))]
        if len(shape) > 1:
            names = " ".join(f"d{i}" for i in range(len(shape)))
            v = v.rearrange(f"p ({names}) -> p {names}",
                            **{f"d{i}": s for i, s in enumerate(shape)})
        return v

    def cf(ar, off, shape):
        v = ar.ap().bitcast(F32)[:, off // 2:off // 2 + int(np.prod(shape))]
        if len(shape) > 1:
            names = " ".join(f"d{i}" for i in range(len(shape)))
            v = v.rearrange(f"p ({names}) -> p {names}",
                            **{f"d{i}": s for i, s in enumerate(shape)})
        return v

    xh1 = cb(arA, 0, [2, 8, 512])
    pbw1r = cb(arA, 8192, [8, 8, 128])
    posr = cb(arA, 16384, [4095])[0:1, :]
    tblsb = cb(arA, 20480, [4095])[0:16, :]
    h0fp = [cb(arB, 0, [8, 512]), cb(arB, 4096, [8, 512])]
    prodr = cf(arB, 0, [1024])[0:1, :]
    mtile = cb(arC, 0, [512])
    rtile = cb(arC, 512, [512])
    rowc = cf(arC, 1024, [3, 512])
    w0row = cb(arC, 5120, [1024])[0:1, :]
    b0row = cb(arC, 7168, [1024])[0:1, :]
    prs0 = cb(arD, 3072, [512])[0:1, :]
    prs1 = cb(arD, 3584, [512])[0:1, :]
    prs2 = cb(arD, 4096, [512])[0:1, :]
    mrowb = cb(arD, 4608, [1024])[0:1, :]
    p0sq = [cb(arC, 4096, [512]), cb(arC, 6144, [512])]
    mrow = cf(arD, 0, [1536])[0:1, :]
    xproj = cb(arA, 0, [12, 2048])
    kpack = cb(arB, 0, [4, 2048])
    woa = cb(arB, 0, [4, 1024])
    RhT = cb(arB, 4096, [2048])[0:8, :]
    wjbuf = cb(arC, 0, [2, 8, 128])
    escr = cb(arC, 2048, [2048])
    escr2 = cb(arC, 4096, [2048])
    hid = cb(arA, 0, [9, 4, 512])
    g_extT = cb(arD, 0, [2048])[0:9, :]
    qT = cb(arA, 0, [8, 2048])
    oddsl = cb(arA, 16384, [2048])
    kvb = cb(arB, 0, [2048])
    kdup = cb(arB, 2048, [2048])
    vtok = cb(arB, 4096, [16, DH + 1])
    enull_p = [cb(arB, 5200, [2048])[64:65, :],
               cb(arA, 16384, [2048])[64:65, :]]
    ywork = cf(arD, 0, [1024])
    invw = cf(arD, 2048, [2048])
    biased2v = cf(arD, 2048, [2048])
    optT = xnT
    boutR = cb(arB, 6144, [1024])[0:1, :]
    pairsel = cb(arB, 7168, [512])[0:8, :]
    selE = cb(arA, 18432, [1024])[0:9, :]
    b2e = cb(arA, 19456, [1024])[0:9, :]
    rbs = cf(arC, 0, [512])[0:64, :]
    invb = cb(arA, 18432, [2048])
    biased = cf(arA, 20480, [2048])
    SWBASE = 2048
    ETBASE = 6144

    def w1sl(buf, m, k):
        off = buf * 4096 + m * 1024 + k * 128
        return arC.ap()[:, off:off + 128]

    def w2sl(buf, hc, dc):
        off = buf * 4096 + hc * 1024 + dc * 512
        return arB.ap()[:, off:off + 512]

    def swv(buf, n):
        return arC.ap()[:, SWBASE + buf * 2048: SWBASE + buf * 2048 + n]

    def etv(buf, a, b):
        return arC.ap()[:, ETBASE + buf * 2048 + a: ETBASE + buf * 2048 + b]

    def sw_rev(buf, qc):
        off = SWBASE + buf * 2048 + (1536 - 512 * qc) + 511
        return bass.AP(arC, off, [[10240, 128], [-1, 512]])

    P.do(G, lambda e: e.iota(posr, [[-1, 4095]], base=2047,
                             channel_multiplier=0,
                             allow_small_or_imprecise_dtypes=True), inc="Gs")
    g_iota0 = P.mark("Gs")
    P.do(G, lambda e: e.dma_start(w0row[:], PbW0), inc="dPC2", by=16)
    P.do(G, lambda e: e.dma_start(b0row[:], B0R), inc="dPC2", by=16)
    P.do(G, lambda e: e.dma_start(pbw2[:], PbW2t), inc="dPC2", by=16)
    P.do(G, lambda e: e.dma_start(ident[:], Ident), inc="dPC2", by=16)
    pc_pb = P.mark("dPC2")
    for dst, src in [(gatesb[:], GateW),
                     (lnbTb[:], LnB), (nulkd[:], NulKD),
                     (vnule[64:65, :], VNulE)]:
        P.do(G, lambda e, d=dst, s=src: e.dma_start(d, s), inc="dPC2", by=16)
    pc_small = P.mark("dPC2")
    pbw1_in = P.do(G, lambda e: e.dma_start(
        pbw1r.rearrange("p a b c -> p (a b c)"),
        PbW1t.rearrange("p a b c -> p (a b c)")), inc="dP1", by=16)

    for dst, src in [(binT[:], BinT), (b1T[:], B1T), (bs1T[:], Bs1T),
                     (rsb[:], RS), (lngT[:], LnG), (nabr[:], NabR),
                     (pbc[:, 0, :], PbB0), (pbc[:, 1, :], PbL0G),
                     (pbc[:, 2, :], PbL0B), (pbc[:, 3, :], PbB1),
                     (pbc[:, 4, :], PbL1G), (pbc[:, 5, :], PbL1B),
                     (pbb2c[:], PbB2), (ident32f[:], Ident)]:
        P.do("sync", lambda e, d=dst, s=src: e.dma_start(d, s), inc="dC", by=16)
    consts_in = P.mark("dC")
    for lt in range(NLT):
        P.do("sync", lambda e, lt=lt: e.dma_start(
            stream[:, lt, :], X[ts(lt), :]), inc="dX", by=16)
    x_in = P.mark("dX")

    for t, v in [(ones1, 1.0), (ones1f, 1.0), (onesc, 1.0), (onesT, 1.0),
                 (cm8, -CSTAB), (epsb, 1e-5)]:
        P.do("vector", lambda e, t=t, v=v: e.memset(t[:], v))
    P.wait("vector", "dC", consts_in)
    v_nab = P.do("vector", lambda e: e.tensor_scalar(
        nabm8[:], nabr[:], -CSTAB, None, op0=Alu.add), inc="V")

    P.wait("tensor", "V", v_nab)
    tb1 = P.do("tensor", lambda e: e.matmul(
        psB[:, 0:1], ones1f[:], rsb[:], start=True, stop=True), inc="T")
    P.wait("vector", "T", tb1)
    consts_built = P.do("vector", lambda e: e.tensor_copy(
        rs_b[:], psB[:, 0:1]), inc="V")

    g_iota = g_iota0

    def ln_tile(lt):
        P.wait("vector", "dX", x_in)
        P.do("vector", lambda e, lt=lt: e.bn_stats(
            bnst[:, 0, :], stream[:, lt, 0:512]))
        v1 = P.do("vector", lambda e, lt=lt: e.bn_stats(
            bnst[:, 1, :], stream[:, lt, 512:1024]), inc="V")
        P.wait("vector", "V", v1)
        v2 = P.do("vector", lambda e: e.bn_aggr(
            lnscr[:, 0:2], bnst[:, 0:2, :].rearrange("p a b -> p (a b)")),
            inc="V")
        P.wait("scalar", "V", v2)
        s0 = P.do("scalar", lambda e: e.activation(
            lnscr[:, 2:3], lnscr[:, 1:2], AF.Ln, bias=epsb[:], scale=1.0),
            inc="S")
        P.wait("scalar", "S", s0)
        s1 = P.do("scalar", lambda e: e.activation(
            lnscr[:, 3:4], lnscr[:, 2:3], AF.Exp, scale=-0.5), inc="S")
        P.wait("vector", "S", s1)
        v3 = P.do("vector", lambda e, lt=lt: e.tensor_scalar(
            xn_tok[:, lt % 2, :], stream[:, lt, :], lnscr[:, 0:1],
            lnscr[:, 3:4], op0=Alu.subtract, op1=Alu.mult), inc="V")
        P.wait("tensor", "V", v3)
        base = 2048 + (lt % 2) * 1024
        for j in range(8):
            P.do("tensor", lambda e, lt=lt, j=j, base=base: e.transpose(
                psB_bf[:, base + j * 128:base + (j + 1) * 128],
                xn_tok[:, lt % 2, ts(j)], ident[:]), inc="T")
        t1 = P.mark("T")
        P.wait("vector", "T", t1)
        return P.do("vector", lambda e, lt=lt, base=base: e.tensor_copy(
            xnT[:, :, lt * 128:(lt + 1) * 128],
            psB_bf[:, base:base + 1024].rearrange("p (a c) -> p a c", a=8)),
            inc="V")

    def ln_tiles_pipe(lts=None):
        if lts is None:
            lts = list(range(NLT))
        am, bm, cm_, dm = {}, {}, {}, {}

        def stA(lt):
            pa = lt % 2
            P.wait("vector", "dX", x_in)
            P.do("vector", lambda e, lt=lt, pa=pa: e.bn_stats(
                bnst[:, 2 * pa, :], stream[:, lt, 0:512]))
            v1 = P.do("vector", lambda e, lt=lt, pa=pa: e.bn_stats(
                bnst[:, 2 * pa + 1, :], stream[:, lt, 512:1024]), inc="V")
            P.wait("vector", "V", v1)
            am[lt] = P.do("vector", lambda e, pa=pa: e.bn_aggr(
                lnscr[:, 4 * pa:4 * pa + 2],
                bnst[:, 2 * pa:2 * pa + 2, :].rearrange("p a b -> p (a b)")),
                inc="V")

        def stB(lt, prev2=None):
            pa = lt % 2
            P.wait("scalar", "V", am[lt])
            if prev2 is not None:
                P.wait("scalar", "V", cm_[prev2])
            s0 = P.do("scalar", lambda e, pa=pa: e.activation(
                lnscr[:, 4 * pa + 2:4 * pa + 3], lnscr[:, 4 * pa + 1:
                                                       4 * pa + 2],
                AF.Ln, bias=epsb[:], scale=1.0), inc="S")
            P.wait("scalar", "S", s0)
            bm[lt] = P.do("scalar", lambda e, pa=pa: e.activation(
                lnscr[:, 4 * pa + 3:4 * pa + 4],
                lnscr[:, 4 * pa + 2:4 * pa + 3], AF.Exp, scale=-0.5),
                inc="S")

        def stC(lt):
            pa = lt % 2
            P.wait("vector", "S", bm[lt])
            cm_[lt] = P.do("vector", lambda e, lt=lt, pa=pa: e.tensor_scalar(
                xn_tok[:, pa, :], stream[:, lt, :], lnscr[:, 4 * pa:
                                                          4 * pa + 1],
                lnscr[:, 4 * pa + 3:4 * pa + 4],
                op0=Alu.subtract, op1=Alu.mult), inc="V")

        def stD(lt):
            base = 2048 + (lt % 2) * 1024
            P.wait("tensor", "V", cm_[lt])
            for j in range(8):
                P.do("tensor", lambda e, lt=lt, j=j, base=base: e.transpose(
                    psB_bf[:, base + j * 128:base + (j + 1) * 128],
                    xn_tok[:, lt % 2, ts(j)], ident[:]), inc="T")
            dm[lt] = P.mark("T")

        def stE(lt):
            base = 2048 + (lt % 2) * 1024
            P.wait("vector", "T", dm[lt])
            return P.do("vector", lambda e, lt=lt, base=base: e.tensor_copy(
                xnT[:, :, lt * 128:(lt + 1) * 128],
                psB_bf[:, base:base + 1024].rearrange("p (a c) -> p a c",
                                                      a=8)), inc="V")

        stA(lts[0])
        stB(lts[0])
        last = 0
        for i, lt in enumerate(lts):
            stC(lt)
            stD(lt)
            if i + 1 < len(lts):
                stA(lts[i + 1])
                stB(lts[i + 1], lts[i - 1] if i >= 1 else None)
            last = stE(lt)
        return last

    P.wait("vector", "dPC2", pc_small)
    cch = P.do("vector", lambda e: e.tensor_tensor(
        prodr, w0row[:], w0row[:], Alu.mult), inc="V")
    P.wait("vector", "V", cch)
    cch = P.do("vector", lambda e: e.tensor_reduce(
        p0sc[0:1, 0:1], prodr, axis=mybir.AxisListType.X, op=Alu.add),
        inc="V")
    P.wait("vector", "V", cch)
    cch = P.do("vector", lambda e: e.tensor_tensor(
        prodr, w0row[:], b0row[:], Alu.mult), inc="V")
    P.wait("vector", "V", cch)
    cch = P.do("vector", lambda e: e.tensor_reduce(
        p0sc[0:1, 1:2], prodr, axis=mybir.AxisListType.X, op=Alu.add),
        inc="V")
    P.wait("vector", "V", cch)
    cch = P.do("vector", lambda e: e.tensor_tensor(
        prodr, b0row[:], b0row[:], Alu.mult), inc="V")
    P.wait("vector", "V", cch)
    cch = P.do("vector", lambda e: e.tensor_reduce(
        p0sc[0:1, 2:3], prodr, axis=mybir.AxisListType.X, op=Alu.add),
        inc="V")
    P.do("vector", lambda e: e.tensor_reduce(
        p0sc[0:1, 3:4], w0row[:], axis=mybir.AxisListType.X, op=Alu.add))
    cch = P.do("vector", lambda e: e.tensor_reduce(
        p0sc[0:1, 4:5], b0row[:], axis=mybir.AxisListType.X, op=Alu.add),
        inc="V")
    P.wait("vector", "V", cch)
    for sl, sc in [(0, 1.0 / 1024), (1, 2.0 / 1024), (2, 1.0 / 1024),
                   (3, 1.0 / 1024), (4, 1.0 / 1024)]:
        cch = P.do("vector", lambda e, sl=sl, sc=sc: e.tensor_scalar(
            p0sc[0:1, sl:sl + 1], p0sc[0:1, sl:sl + 1], sc, None,
            op0=Alu.mult), inc="V")
    p0c_done = P.mark("V")

    rowA = rowc[0:1, 0, :]
    rowB = rowc[0:1, 1, :]
    rowC = rowc[0:1, 2, :]

    def posmlp():
        P.wait("tensor", "dPC2", pc_small)
        P.wait("tensor", "dP1", pbw1_in)
        P.wait("tensor", "V", consts_built)
        P.wait("scalar", "dC", consts_in)
        P.wait("vector", "Gs", g_iota)
        n = 512
        tl2d = {}
        l0rd = {}
        evd = {}
        cpyd = {}
        l2evd = {}
        sqm = {0: 0, 1: 0}
        rowm_d = {}
        sx1_d = {}
        sx2_d = {}
        tst_d = {}
        nrm_d = {}
        bcd = {}

        def st_rows(rc):
            r0 = rc * 512
            pos_n = posr[0:1, r0:r0 + n]
            if rc > 0:
                P.wait("vector", "T", l0rd[rc - 1])
            else:
                P.wait("vector", "V", p0c_done)
            q1 = P.do("vector", lambda e, pos_n=pos_n: e.tensor_tensor(
                rowA[:, 0:n], pos_n, pos_n, Alu.mult), inc="V")
            P.do("vector", lambda e, pos_n=pos_n: e.tensor_scalar(
                rowB[:, 0:n], pos_n, p0sc[0:1, 1:2], p0sc[0:1, 2:3],
                op0=Alu.mult, op1=Alu.add))
            P.wait("vector", "V", q1)
            q3 = P.do("vector", lambda e: e.tensor_scalar(
                rowA[:, 0:n], rowA[:, 0:n], p0sc[0:1, 0:1], None,
                op0=Alu.mult), inc="V")
            P.wait("vector", "V", q3)
            q4 = P.do("vector", lambda e: e.tensor_tensor(
                rowB[:, 0:n], rowB[:, 0:n], rowA[:, 0:n], Alu.add), inc="V")
            q5 = P.do("vector", lambda e, pos_n=pos_n: e.tensor_scalar(
                rowC[:, 0:n], pos_n, p0sc[0:1, 3:4], p0sc[0:1, 4:5],
                op0=Alu.mult, op1=Alu.add), inc="V")
            P.wait("vector", "V", q5)
            q6 = P.do("vector", lambda e: e.scalar_tensor_tensor(
                rowA[:, 0:n], rowC[:, 0:n], 1.0, rowC[:, 0:n],
                op0=Alu.mult, op1=Alu.mult), inc="V")
            P.wait("vector", "V", q6)
            q7 = P.do("vector", lambda e: e.tensor_tensor(
                rowA[:, 0:n], rowB[:, 0:n], rowA[:, 0:n], Alu.subtract),
                inc="V")
            P.wait("scalar", "V", q7)
            sr1 = P.do("scalar", lambda e: e.activation(
                rowB[:, 0:n], rowA[:, 0:n], AF.Ln, bias=epsb[0:1, :],
                scale=1.0), inc="S")
            P.wait("scalar", "S", sr1)
            sr2 = P.do("scalar", lambda e: e.activation(
                rowA[:, 0:n], rowB[:, 0:n], AF.Exp, scale=-0.5), inc="S")
            P.wait("vector", "S", sr2)
            P.do("vector", lambda e, pos_n=pos_n: e.tensor_tensor(
                prs0[:, 0:n], pos_n, rowA[:, 0:n], Alu.mult))
            P.do("vector", lambda e: e.tensor_scalar(
                prs1[:, 0:n], rowA[:, 0:n], 1.0, None, op0=Alu.mult))
            rowm_d[rc] = P.do("vector", lambda e: e.scalar_tensor_tensor(
                prs2[:, 0:n], rowC[:, 0:n], -1.0, rowA[:, 0:n],
                op0=Alu.mult, op1=Alu.mult), inc="V")

        def st_l0(rc):
            par = rc % 2
            psX = psA if par == 0 else psB
            P.wait("tensor", "V", rowm_d[rc])
            if rc >= 2:
                P.wait("tensor", "V", evd[(rc - 2, 7)])
            sl0 = {}
            for fc in range(8):
                bank = fc % 2
                if fc >= 2:
                    P.wait("tensor", "S", sl0[fc - 2])
                P.do("tensor", lambda e, fc=fc, bank=bank, psX=psX:
                     e.matmul(psX[:, bank * 512:bank * 512 + n],
                              w0row[:, ts(fc)], prs0[:, 0:n],
                              start=True, stop=False), inc="T")
                P.do("tensor", lambda e, fc=fc, bank=bank, psX=psX:
                     e.matmul(psX[:, bank * 512:bank * 512 + n],
                              b0row[:, ts(fc)], prs1[:, 0:n],
                              start=False, stop=False), inc="T")
                P.do("tensor", lambda e, bank=bank, psX=psX:
                     e.matmul(psX[:, bank * 512:bank * 512 + n],
                              ones1[:], prs2[:, 0:n],
                              start=False, stop=True), inc="T")
                tm0 = P.mark("T")
                P.wait("scalar", "T", tm0)
                if fc == 0 and rc >= 2:
                    P.wait("scalar", "T", tl2d[rc - 2])
                sl0[fc] = P.do("scalar", lambda e, fc=fc, bank=bank,
                               psX=psX, par=par: e.activation(
                    xh1[:, par, fc, 0:n], psX[:, bank * 512:bank * 512 + n],
                    AF.Silu, bias=pbc[:, 2, fc:fc + 1],
                    scale=pbc[:, 1, fc:fc + 1]), inc="S")
            l0rd[rc] = P.mark("T")
            sx1_d[rc] = P.mark("S")

        def st_l1(rc):
            par = rc % 2
            psX = psA if par == 0 else psB
            h0 = h0fp[par]
            P.wait("tensor", "S", sx1_d[rc])
            for fc in range(8):
                bank = fc % 2
                if fc >= 2:
                    P.wait("tensor", "V", evd[(rc, fc - 2)])
                for k in range(8):
                    P.do("tensor", lambda e, fc=fc, k=k, bank=bank,
                         par=par, psX=psX: e.matmul(
                        psX[:, bank * 512:bank * 512 + n],
                        pbw1r[:, fc, k, :], xh1[:, par, k, 0:n],
                        start=(k == 0), stop=(k == 7)), inc="T")
                tmm = P.mark("T")
                P.wait("vector", "T", tmm)
                evd[(rc, fc)] = P.do("vector", lambda e, fc=fc,
                                     bank=bank, psX=psX, h0=h0:
                                     e.tensor_scalar(
                    h0[:, fc, 0:n], psX[:, bank * 512:bank * 512 + n],
                    pbc[:, 3, fc:fc + 1], None, op0=Alu.add), inc="V")

        def st_stats(rc):
            h0 = h0fp[rc % 2]
            for fc in range(8):
                pb2 = fc % 2
                P.wait("scalar", "V", evd[(rc, fc)])
                if fc >= 2:
                    P.wait("scalar", "T", sqm[pb2])
                sm = P.do("scalar", lambda e, fc=fc, pb2=pb2, h0=h0:
                          e.activation(p0sq[pb2][:, 0:n], h0[:, fc, 0:n],
                                       AF.Square), inc="S")
                if fc == 0:
                    P.wait("tensor", "V", cpyd.get(rc - 1, 0))
                    P.wait("tensor", "S", l2evd.get(rc - 1, 0))
                P.wait("tensor", "V", evd[(rc, fc)])
                P.do("tensor", lambda e, fc=fc, h0=h0: e.matmul(
                    psA[0:1, 1024:1024 + n], onesc[:], h0[:, fc, 0:n],
                    start=(fc == 0), stop=(fc == 7)), inc="T")
                P.wait("tensor", "S", sm)
                sqm[pb2] = P.do("tensor", lambda e, fc=fc, pb2=pb2:
                                e.matmul(
                    psA[0:1, 1536:1536 + n], onesc[:], p0sq[pb2][:, 0:n],
                    start=(fc == 0), stop=(fc == 7)), inc="T")
            tst_d[rc] = P.mark("T")

        def st_mrow(rc):
            P.wait("vector", "T", tst_d[rc])
            P.do("vector", lambda e: e.tensor_scalar(
                mrow[0:1, 0:n], psA[0:1, 1024:1024 + n], 1.0 / 1024.0, None,
                op0=Alu.mult))
            u1 = P.do("vector", lambda e: e.tensor_scalar(
                mrow[0:1, 512:512 + n], psA[0:1, 1536:1536 + n], 1.0 / 1024.0,
                None, op0=Alu.mult), inc="V")
            P.wait("vector", "V", u1)
            u2 = P.do("vector", lambda e: e.tensor_tensor(
                mrow[0:1, 1024:1024 + n], mrow[0:1, 0:n], mrow[0:1, 0:n],
                Alu.mult), inc="V")
            P.wait("vector", "V", u2)
            u3 = P.do("vector", lambda e: e.tensor_tensor(
                mrow[0:1, 512:512 + n], mrow[0:1, 512:512 + n],
                mrow[0:1, 1024:1024 + n], Alu.subtract), inc="V")
            P.wait("scalar", "V", u3)
            w1s = P.do("scalar", lambda e: e.activation(
                mrow[0:1, 1024:1024 + n], mrow[0:1, 512:512 + n], AF.Ln,
                bias=epsb[0:1, :], scale=1.0), inc="S")
            P.wait("scalar", "S", w1s)
            w2s = P.do("scalar", lambda e: e.activation(
                mrow[0:1, 512:512 + n], mrow[0:1, 1024:1024 + n], AF.Exp,
                scale=-0.5), inc="S")
            if rc >= 2:
                P.wait("vector", "dSW0", bcd[rc - 2])
            P.do("vector", lambda e, n=n: e.tensor_scalar(
                mrowb[0:1, 0:n], mrow[0:1, 0:n], 1.0, None, op0=Alu.mult))
            P.wait("vector", "S", w2s)
            cmr = P.do("vector", lambda e, n=n: e.tensor_scalar(
                mrowb[0:1, 512:512 + n], mrow[0:1, 512:512 + n], 1.0, None,
                op0=Alu.mult), inc="V")
            P.wait(G, "V", cmr)
            if rc >= 2:
                P.wait(G, "V", nrm_d[rc - 2])
            P.do(G, lambda e, n=n: e.dma_start(
                mtile[:, 0:n],
                bass.AP(arD, 4608, [[6144, 1], [0, 128], [1, n]])),
                inc="dSW0", by=16)
            bcd[rc] = P.do(G, lambda e, n=n: e.dma_start(
                rtile[:, 0:n],
                bass.AP(arD, 5120, [[6144, 1], [0, 128], [1, n]])),
                inc="dSW0", by=16)
            cpyd[rc] = P.mark("V")
            P.wait("vector", "dSW0", bcd[rc])

        def st_norm(rc):
            par = rc % 2
            h0 = h0fp[par]
            for fc in range(8):
                v3 = P.do("vector", lambda e, fc=fc, h0=h0:
                          e.tensor_tensor(h0[:, fc, 0:n], h0[:, fc, 0:n],
                                          mtile[:, 0:n], Alu.subtract),
                          inc="V")
                P.wait("vector", "V", v3)
                vn = P.do("vector", lambda e, fc=fc, h0=h0:
                          e.tensor_tensor(h0[:, fc, 0:n], h0[:, fc, 0:n],
                                          rtile[:, 0:n], Alu.mult), inc="V")
                P.wait("scalar", "V", vn)
                P.do("scalar", lambda e, fc=fc, h0=h0, par=par:
                     e.activation(xh1[:, par, fc, 0:n], h0[:, fc, 0:n],
                                  AF.Silu, bias=pbc[:, 5, fc:fc + 1],
                                  scale=pbc[:, 4, fc:fc + 1]), inc="S")
                nrm_d[rc] = vn
            sx2_d[rc] = P.mark("S")

        def st_l2(rc):
            r0 = rc * 512
            par = rc % 2
            P.wait("tensor", "S", sx2_d[rc])
            for k in range(8):
                P.do("tensor", lambda e, k=k, par=par: e.matmul(
                    psA[0:16, 1024:1024 + n], pbw2[:, k, :],
                    xh1[:, par, k, 0:n],
                    start=(k == 0), stop=(k == 7)), inc="T")
            tl2d[rc] = P.mark("T")
            P.wait("scalar", "T", tl2d[rc])
            l2evd[rc] = P.do("scalar", lambda e, r0=r0: e.activation(
                tblsb[0:16, r0:r0 + n], psA[0:16, 1024:1024 + n], AF.Identity,
                bias=pbb2c[0:16, :], scale=1.0), inc="S")

        st_rows(0)
        st_l0(0)
        for rc in range(4):
            st_l1(rc)
            if rc + 1 < 4:
                st_rows(rc + 1)
            st_stats(rc)
            if rc + 1 < 4:
                st_l0(rc + 1)
            st_mrow(rc)
            st_norm(rc)
            st_l2(rc)
            lt_sched = {0: list(range(0, 6)), 1: list(range(6, 12)),
                        2: list(range(12, 16)), 3: []}[rc]
            if lt_sched:
                ln_tiles_pipe(lts=lt_sched)
        vmsk = P.do("vector", lambda e: e.memset(
            tblsb[0:16, 2048:4095], -1e9), inc="V")
        stbl = P.mark("S")
        P.wait(G, "S", stbl)
        P.wait(G, "V", vmsk)
        P.do(G, lambda e: e.dma_start(TBL, tblsb), inc="dT", by=16)
    posmlp()
    pos_T = P.mark("T")
    pos_S = P.mark("S")
    pos_V = P.mark("V")
    tbl_done = P.mark("dT")

    def attn_a():
        P.wait("vector", "dX", x_in)
        P.wait("tensor", "V", consts_built)
        xnT_done = P.mark("V")
        if _SUB <= 1:
            return P.mark("V")
        P.wait("vector", "dT", tbl_done)

        P.wait(G, "V", pos_V)
        P.wait(G, "S", pos_S)
        res_prev = {}
        exp_last = {0: 0, 1: 0}
        for half in range(2):
            if half == 1:
                for d_, s_ in [(W1_b, W1t), (Ws1_b, Ws1t),
                               (W2_b, W2t), (Ws2_b, Ws2t)]:
                    P.do(G, lambda e, d_=d_, s_=s_: e.dma_start(d_, s_),
                         inc="dPC3", by=16)
            ev = {}
            mmdone = {}
            for jj in range(12):
                j = half * 12 + jj
                buf = jj % 2
                wsem = "dW0" if buf == 0 else "dW1"
                gidx = half * 12 + jj
                if gidx >= 2:
                    P.wait(G, "T", mmdone.get(gidx - 2, 0))
                wm = P.do(G, lambda e, j=j, buf=buf: e.dma_start(
                    wjbuf[:, buf, :, :], WinT[j]), inc=wsem, by=16)
                P.wait("tensor", wsem, wm)
                if jj == 0:
                    if half == 0:
                        P.wait("tensor", "V", xnT_done)
                        P.wait("tensor", "T", pos_T)
                        P.wait("tensor", "S", pos_S)
                    else:
                        P.wait("tensor", "S", exp_last[0])
                        P.wait("tensor", "S", exp_last[1])
                        P.wait("tensor", "V", P.mark("V"))
                for lsub in range(4):
                    pb = lsub % 2
                    if ("ev", pb) in ev:
                        P.wait("tensor", "V", ev[("ev", pb)])
                    for k in range(8):
                        P.do("tensor", lambda e, buf=buf, k=k, lsub=lsub,
                             pb=pb: e.matmul(
                            psB[:, pb * 512:pb * 512 + 512],
                            wjbuf[:, buf, k, :], xnT[:, k, ts(lsub, 512)],
                            start=(k == 0), stop=(k == 7)), inc="T")
                    tm = P.mark("T")
                    if lsub == 3:
                        mmdone[gidx] = tm
                    P.wait("vector", "T", tm)
                    em = P.do("vector", lambda e, j=j, jj=jj, lsub=lsub,
                              pb=pb: e.tensor_scalar(
                        xproj[:, jj, ts(lsub, 512)],
                        psB[:, pb * 512:pb * 512 + 512],
                        binT[:, j:j + 1], None, op0=Alu.add), inc="V")
                    ev[("ev", pb)] = em
                    if lsub == 3:
                        ev[("chunk", jj)] = em
            if _SUB <= 2:
                return P.mark("V")
            kp = {}
            for pp in range(4):
                c0 = 3 * pp
                P.wait(G, "V", ev[("chunk", c0)])
                P.wait(G, "V", ev[("chunk", c0 + 2)])
                P.do(G, lambda e, pp=pp, c0=c0: e.dma_start(
                    kpack[0:64, pp, :], xproj[64:128, c0, :]), inc="dK", by=16)
                kp[pp] = P.do(G, lambda e, pp=pp, c0=c0: e.dma_start(
                    kpack[64:128, pp, :], xproj[0:64, c0 + 2, :]),
                    inc="dK", by=16)
            kp_all = P.mark("dK")
            if _SUB <= 3:
                return P.mark("V")
            P.wait("tensor", "dK", kp_all)
            P.wait("tensor", "V", ev[("ev", 0)])
            P.wait("tensor", "V", ev[("ev", 1)])
            for pp in range(4):
                c0 = 3 * pp
                for qt in range(16):
                    for par in range(2):
                        hh = 2 * pp + par
                        qsl = c0 if par == 0 else c0 + 1
                        base = 0 if par == 0 else 64
                        ps = psA if par == 0 else psB
                        if exp_last[par]:
                            P.wait("tensor", "S", exp_last[par])
                        for mc in range(4):
                            P.do("tensor", lambda e, qsl=qsl, base=base,
                                 qt=qt, pp=pp, mc=mc, ps=ps: e.matmul(
                                ps[:, ts(mc, 512)],
                                xproj[base:base + 64, qsl, ts(qt)],
                                kpack[base:base + 64, pp, ts(mc, 512)],
                                start=True, stop=True), inc="T")
                        tm = P.mark("T")
                        P.wait("scalar", "T", tm)
                        edst = escr if par == 0 else escr2
                        exp_last[par] = P.do(
                            "scalar", lambda e, ps=ps, qt=qt, hh=hh,
                            edst=edst:
                            e.activation(edst, ps[:, :], AF.Exp, bias=cm8[:],
                                         scale=0.125,
                                         accum_out=Sa[:, hh * 16 + qt:
                                                      hh * 16 + qt + 1]),
                            inc="S")
            s_exp = P.mark("S")
            if _SUB <= 4:
                return P.mark("V")

            P.wait("vector", "S", s_exp)
            v1 = P.do("vector", lambda e: e.reciprocal(Ra[:], Sa[:]), inc="V")
            P.wait("vector", "V", v1)
            v2 = P.do("vector", lambda e: e.tensor_tensor(
                Ra[:], Sa[:], Ra[:], Alu.mult), inc="V")
            P.wait("tensor", "V", v2)
            for qt in range(16):
                if qt == 8:
                    P.wait("tensor", "V", P.mark("V"))
                src = bass.AP(Ra, qt, [[128, 128], [16, 8]])
                P.do("tensor", lambda e, qt=qt, src=src: e.transpose(
                    psA[0:8, (qt % 8) * 128:(qt % 8) * 128 + 128], src,
                    ident32f[:]), inc="T")
                if qt % 8 == 7:
                    trm = P.mark("T")
                    P.wait("vector", "T", trm)
                    for q2 in range(qt - 7, qt + 1):
                        P.do("vector", lambda e, q2=q2: e.tensor_copy(
                            RhT[0:8, ts(q2)],
                            psA[0:8, (q2 % 8) * 128:(q2 % 8) * 128 + 128]),
                            inc="V")
            rat = P.mark("V")

            P.wait(G, "S", s_exp)
            P.do(G, lambda e: e.dma_start(pairsel, PairSel), inc="dK", by=16)
            if half == 0:
                P.do(G, lambda e: e.dma_start(boutR, BoutR), inc="dK", by=16)
            ab_consts = P.mark("dK")
            P.wait("tensor", "dK", ab_consts)
            vs = {}
            cpq2 = {0: P.mark("V"), 1: rat}
            for lc in range(4):
                c_ready = cpq2[0] if lc < 2 else rat
                for pp in range(4):
                    c0 = 3 * pp
                    idx = lc * 4 + pp
                    bank = idx % 2
                    if idx == 0:
                        P.wait("tensor", "V", cpq2[0])
                    if idx == 8:
                        P.wait("tensor", "V", rat)
                    if idx >= 2:
                        P.wait("tensor", "V", vs.get(idx - 2, 0))
                    P.do("tensor", lambda e, pp=pp, bank=bank, lc=lc:
                         e.matmul(psB[:, bank * 512:bank * 512 + 512],
                                  pairsel[0:8, ts(pp)], RhT[0:8, ts(lc, 512)],
                                  start=True, stop=True), inc="T")
                    tm = P.mark("T")
                    P.wait("vector", "T", tm)
                    P.do("vector", lambda e, c0=c0, lc=lc, bank=bank:
                         e.tensor_tensor(
                        xproj[0:64, c0, ts(lc, 512)],
                        xproj[0:64, c0 + 1, ts(lc, 512)],
                        psB[0:64, bank * 512:bank * 512 + 512], Alu.mult),
                        inc="V")
                    vs[idx] = P.do(
                        "vector", lambda e, c0=c0, lc=lc, bank=bank:
                        e.tensor_tensor(
                            xproj[64:128, c0, ts(lc, 512)],
                            xproj[64:128, c0 + 2, ts(lc, 512)],
                            psB[64:128, bank * 512:bank * 512 + 512],
                            Alu.mult), inc="V")
            vsc_done = P.mark("V")
            if _SUB <= 5:
                return P.mark("V")

            P.wait(G, "S", s_exp)
            wm = P.do(G, lambda e, half=half: e.dma_start(
                woa, WoutT[4 * half:4 * half + 4]
                .rearrange("a p c -> p a c")), inc="dR", by=16)
            P.wait("tensor", "dR", wm)
            for ltg in range(4):
                for lt4 in range(4):
                    lt = ltg * 4 + lt4
                    for dc in range(2):
                        b = lt4 * 2 + dc
                        ps, pof = (psA, b) if b < 4 else (psB, b - 4)
                        P.wait("tensor", "V",
                               vsc_done if b in (4, 5) else vs[ltg * 4 + 3])
                        key = (lt4, dc)
                        if key in res_prev:
                            P.wait("tensor", "V", res_prev[key])
                        nmm = 4 + (1 if half == 0 else 0)
                        cnt = 0
                        for pp in range(4):
                            c0 = 3 * pp
                            cnt += 1
                            P.do("tensor", lambda e, c0=c0, lt=lt, pp=pp,
                                 dc=dc, ps=ps, pof=pof, st=(cnt == 1),
                                 sp=(cnt == nmm): e.matmul(
                                ps[:, pof * 512:pof * 512 + 512],
                                xproj[:, c0, ts(lt)],
                                woa[:, pp, ts(dc, 512)],
                                start=st, stop=sp), inc="T")
                        if half == 0:
                            P.do("tensor", lambda e, dc=dc, ps=ps, pof=pof:
                                 e.matmul(
                                ps[:, pof * 512:pof * 512 + 512], ones1[:],
                                boutR[0:1, ts(dc, 512)],
                                start=False, stop=True), inc="T")
                        tm = P.mark("T")
                        P.wait("vector", "T", tm)
                        res_prev[key] = P.do(
                            "vector", lambda e, lt=lt, dc=dc, ps=ps, pof=pof:
                            e.scalar_tensor_tensor(
                                stream[:, lt, ts(dc, 512)],
                                ps[:, pof * 512:pof * 512 + 512], 1.0,
                                stream[:, lt, ts(dc, 512)],
                                op0=Alu.mult, op1=Alu.add), inc="V")
        return P.mark("V")

    def ffn(tag):
        P.wait("sync", "T", P.mark("T"))
        P.wait("sync", "V", P.mark("V"))
        P.wait(G, "T", P.mark("T"))
        P.wait(G, "V", P.mark("V"))
        P.do(G, lambda e: e.dma_start(selE, SelE), inc="dK", by=16)
        P.do(G, lambda e: e.dma_start(b2e, B2E), inc="dK", by=16)
        P.do(G, lambda e: e.dma_start(g_extT[8:9, :], OnesR), inc="dK", by=16)
        feconst = P.mark("dK")
        ones_in = feconst
        ln_tiles_pipe()
        xnT_done = P.mark("V")
        gprev = 0
        P.wait("tensor", "V", xnT_done)
        P.wait("tensor", "dK", ones_in)
        for lt in range(NLT):
            if gprev:
                P.wait("tensor", "V", gprev)
            for k in range(8):
                P.do("tensor", lambda e, lt=lt, k=k: e.matmul(
                    psB[:, 0:8], xnT[:, k, ts(lt)], gatesb[:, k, :],
                    start=(k == 0), stop=(k == 7)), inc="T")
            tg = P.mark("T")
            P.wait("scalar", "T", tg)
            sg = P.do("scalar", lambda e: e.activation(
                gwk[:, 0:8], psB[:, 0:8], AF.Sigmoid), inc="S")
            P.wait("vector", "S", sg)
            seq = [
                lambda e: e.tensor_scalar(gwk[:, 8:16], gwk[:, 0:8],
                                          rs_b[:, 0:1], None, op0=Alu.mult),
                lambda e: e.tensor_reduce(gwk[:, 16:17], gwk[:, 8:16],
                                          axis=mybir.AxisListType.X,
                                          op=Alu.max),
                lambda e: e.tensor_scalar(gwk[:, 0:8], gwk[:, 8:16],
                                          gwk[:, 16:17], None,
                                          op0=Alu.is_equal),
                lambda e: e.scalar_tensor_tensor(gwk[:, 0:8], gwk[:, 0:8],
                                                 -1e38, gwk[:, 8:16],
                                                 op0=Alu.mult, op1=Alu.add),
                lambda e: e.tensor_reduce(gwk[:, 17:18], gwk[:, 0:8],
                                          axis=mybir.AxisListType.X,
                                          op=Alu.max),
                lambda e: e.tensor_tensor(gwk[:, 18:19], gwk[:, 16:17],
                                          gwk[:, 17:18], Alu.add),
                lambda e: e.reciprocal(gwk[:, 19:20], gwk[:, 18:19]),
                lambda e: e.tensor_scalar(gwk[:, 0:8], gwk[:, 8:16],
                                          gwk[:, 17:18], None, op0=Alu.is_ge),
                lambda e: e.scalar_tensor_tensor(gtok[:, 0:8], gwk[:, 8:16],
                                                 gwk[:, 19:20], gwk[:, 0:8],
                                                 op0=Alu.mult, op1=Alu.mult),
            ]
            vg = 0
            for f in seq:
                vg = P.do("vector", f, inc="V")
                P.wait("vector", "V", vg)
            P.wait("tensor", "V", vg)
            tt = P.do("tensor", lambda e: e.transpose(
                psB_bf[0:8, 16:144], gtok[:, 0:8], ident[:]), inc="T")
            P.wait("vector", "T", tt)
            gprev = P.do("vector", lambda e, lt=lt: e.tensor_copy(
                g_extT[0:8, ts(lt)], psB_bf[0:8, 16:144]), inc="V")
        g_all = P.mark("V")

        w1m = {}
        res = {}
        for lc in range(4):
            scprev = {}
            for e_i in range(9):
                buf = e_i % 2
                wsA = "dU0" if buf == 0 else "dU2"
                wsB = "dU1" if buf == 0 else "dU3"
                gi = lc * 9 + e_i
                if gi >= 2:
                    P.wait(G, "T", w1m.get(gi - 2, 0))
                if gi == 0:
                    P.wait(G, "dPC3", 64)
                src = (W1_b[e_i] if e_i < 8 else Ws1_b)
                wmA = P.do(G, lambda e, src=src, buf=buf: e.dma_start(
                    arC.ap()[:, buf * 4096:buf * 4096 + 2048],
                    src[:, 0:2].rearrange("p a b c -> p (a b c)")),
                    inc=wsA, by=16)
                wmB = P.do(G, lambda e, src=src, buf=buf: e.dma_start(
                    arC.ap()[:, buf * 4096 + 2048:(buf + 1) * 4096],
                    src[:, 2:4].rearrange("p a b c -> p (a b c)")),
                    inc=wsB, by=16)
                if lc == 0 and e_i == 0:
                    P.wait("tensor", "V", g_all)
                    P.wait("tensor", "dK", feconst)
                if lc > 0 and e_i == 0:
                    for b in range(8):
                        P.wait("tensor", "V", res[(lc - 1, b)])
                if e_i < 8:
                    if scprev:
                        P.wait("tensor", "V", scprev.get(3, 0))
                    P.do("tensor", lambda e, e_i=e_i, lc=lc: e.matmul(
                        psB[:, 0:512], selE[0:9, ts(e_i)],
                        g_extT[0:9, ts(lc, 512)], start=True, stop=True),
                        inc="T")
                for m in range(4):
                    P.wait("tensor", wsA if m < 2 else wsB,
                           wmA if m < 2 else wmB)
                    if gi >= 1:
                        P.wait("tensor", "S", w1m.get(("gel", m), 0))
                    for k in range(8):
                        P.do("tensor", lambda e, m=m, k=k, buf=buf, lc=lc:
                             e.matmul(psA[:, ts(m, 512)], w1sl(buf, m, k),
                                      xnT[:, k, ts(lc, 512)],
                                      start=(k == 0), stop=(k == 7)), inc="T")
                    tm = P.mark("T")
                    if m == 3:
                        w1m[gi] = tm
                    P.wait("scalar", "T", tm)
                    bias = (b1T, e_i * 4 + m) if e_i < 8 else (bs1T, m)
                    sm = P.do("scalar", lambda e, m=m, e_i=e_i, bias=bias:
                              e.activation(hid[:, e_i, m, :],
                                           psA[:, ts(m, 512)], AF.Gelu,
                                           bias=bias[0][:, bias[1]:
                                                        bias[1] + 1],
                                           scale=1.0), inc="S")
                    w1m[("gel", m)] = sm
                    if e_i < 8:
                        P.wait("vector", "S", sm)
                        scprev[m] = P.do(
                            "vector", lambda e, m=m, e_i=e_i: e.tensor_tensor(
                                hid[:, e_i, m, :], hid[:, e_i, m, :],
                                psB[:, 0:512], Alu.mult), inc="V")
                        w1m[("sc", m)] = scprev[m]
            s_hid = P.mark("S")
            v_hid = P.mark("V")
            w2m = {}
            for e_i in range(9):
                buf = e_i % 2
                wsA = "dSW0" if buf == 0 else "dSW2"
                wsB = "dSW1" if buf == 0 else "dSW3"
                if e_i == 0:
                    P.wait(G, "dPC3", 64)
                if e_i >= 2:
                    P.wait(G, "T", w2m.get(e_i - 2, 0))
                src = (W2_b[e_i] if e_i < 8 else Ws2_b)
                wmA = P.do(G, lambda e, src=src, buf=buf: e.dma_start(
                    arB.ap()[:, buf * 4096:buf * 4096 + 2048],
                    src[:, 0:2].rearrange("p a c -> p (a c)")),
                    inc=wsA, by=16)
                wmB = P.do(G, lambda e, src=src, buf=buf: e.dma_start(
                    arB.ap()[:, buf * 4096 + 2048:(buf + 1) * 4096],
                    src[:, 2:4].rearrange("p a c -> p (a c)")),
                    inc=wsB, by=16)
                if e_i == 0:
                    P.wait("tensor", "S", s_hid)
                    P.wait("tensor", "V", v_hid)
                for hc in range(4):
                    P.wait("tensor", wsA if hc < 2 else wsB,
                           wmA if hc < 2 else wmB)
                    for b in range(8):
                        lt4, dc = b // 2, b % 2
                        ps, pof = (psA, b) if b < 4 else (psB, b - 4)
                        P.do("tensor", lambda e, e_i=e_i, hc=hc, lt4=lt4,
                             dc=dc, ps=ps, pof=pof, buf=buf: e.matmul(
                            ps[:, pof * 512:pof * 512 + 512],
                            hid[:, e_i, hc, ts(lt4)], w2sl(buf, hc, dc),
                            start=(e_i == 0 and hc == 0), stop=False),
                            inc="T")
                w2m[e_i] = P.mark("T")
            for b in range(8):
                lt4, dc = b // 2, b % 2
                ps, pof = (psA, b) if b < 4 else (psB, b - 4)
                P.do("tensor", lambda e, lc=lc, lt4=lt4, dc=dc, ps=ps,
                     pof=pof: e.matmul(
                    ps[:, pof * 512:pof * 512 + 512],
                    g_extT[0:9, lc * 512 + lt4 * 128:
                           lc * 512 + (lt4 + 1) * 128],
                    b2e[0:9, ts(dc, 512)], start=False, stop=True), inc="T")
                tm = P.mark("T")
                P.wait("vector", "T", tm)
                res[(lc, b)] = P.do(
                    "vector", lambda e, lc=lc, lt4=lt4, dc=dc, ps=ps,
                    pof=pof: e.scalar_tensor_tensor(
                        stream[:, lc * 4 + lt4, ts(dc, 512)],
                        ps[:, pof * 512:pof * 512 + 512], 1.0,
                        stream[:, lc * 4 + lt4, ts(dc, 512)],
                        op0=Alu.mult, op1=Alu.add), inc="V")
            if tag == "f2":
                P.wait("sync", "V", P.mark("V"))
                for lt4 in range(4):
                    P.do("sync", lambda e, lc=lc, lt4=lt4: e.dma_start(
                        OUT[ts(lc * 4 + lt4), :], stream[:, lc * 4 + lt4, :]),
                        inc="dO", by=16)
        return P.mark("V")

    def attn_b():
        P.wait("sync", "T", P.mark("T"))
        wo_pc = P.do(G, lambda e: e.dma_start(Wo_b, WoM), inc="dPC", by=16)
        P.wait(G, "T", P.mark("T"))
        ln_tiles_pipe()
        xnT_done = P.mark("V")

        qm = {}
        bqprev = 0
        for o in range(9):
            buf = o % 2
            wsem = "dW0" if buf == 0 else "dW1"
            if o >= 2:
                P.wait(G, "T", qm.get(("mm", o - 2), 0))
            src = Wqt[o] if o < 8 else Wkvt
            wm = P.do(G, lambda e, src=src, buf=buf: e.dma_start(
                wjbuf[:, buf, :, :], src), inc=wsem, by=16)
            P.wait("tensor", wsem, wm)
            if o == 0:
                P.wait("tensor", "V", xnT_done)
            if bqprev:
                P.wait("tensor", "V", bqprev)
            for k in range(8):
                P.do("tensor", lambda e, k=k, buf=buf: e.matmul(
                    psB[:, 0:1], wjbuf[:, buf, k, :], lnbTb[:, k:k + 1],
                    start=(k == 0), stop=(k == 7)), inc="T")
            tbq = P.mark("T")
            P.wait("vector", "T", tbq)
            dstb = invq[:, o:o + 1] if o < 8 else bkv[:, 0:1]
            bqprev = P.do("vector", lambda e, dstb=dstb: e.tensor_copy(
                dstb, psB[:, 0:1]), inc="V")
            vscl = 0
            for k in range(8):
                vscl = P.do("vector", lambda e, k=k, buf=buf: e.tensor_scalar(
                    wjbuf[:, buf, k, :], wjbuf[:, buf, k, :],
                    lngT[:, k:k + 1], None, op0=Alu.mult), inc="V")
            P.wait("tensor", "V", vscl)
            for lsub in range(4):
                for k in range(8):
                    P.do("tensor", lambda e, k=k, lsub=lsub, buf=buf:
                         e.matmul(psA[:, ts(lsub, 512)], wjbuf[:, buf, k, :],
                                  xnT[:, k, ts(lsub, 512)],
                                  start=(k == 0), stop=(k == 7)), inc="T")
            qm[("mm", o)] = P.mark("T")
            P.wait("vector", "T", qm[("mm", o)])
            for lsub in range(4):
                if o < 8:
                    qm[("ev", o)] = P.do(
                        "vector", lambda e, o=o, lsub=lsub: e.tensor_scalar(
                            qT[:, o, ts(lsub, 512)], psA[:, ts(lsub, 512)],
                            invq[:, o:o + 1], 0.125,
                            op0=Alu.add, op1=Alu.mult), inc="V")
                else:
                    qm[("ev", o)] = P.do(
                        "vector", lambda e, lsub=lsub: e.tensor_scalar(
                            kvb[:, ts(lsub, 512)], psA[:, ts(lsub, 512)],
                            bkv[:, 0:1], None, op0=Alu.add), inc="V")
        kv_done = qm[("ev", 8)]

        P.wait(G, "V", kv_done)
        kd = P.do(G, lambda e: e.dma_start(
            kdup[64:128, :], kvb[0:64, :]), inc="dK", by=16)
        P.wait("tensor", "V", kv_done)
        P.do("vector", lambda e: e.memset(vtok[:, :, DH:DH + 1], 1.0))
        for mt in range(16):
            if mt == 8:
                P.wait("tensor", "V", P.mark("V"))
            P.do("tensor", lambda e, mt=mt: e.transpose(
                psB_bf[:, 2048 + (mt % 8) * 64:2048 + (mt % 8) * 64 + 64],
                kvb[64:128, ts(mt)], ident[64:128, 64:128]), inc="T")
            if mt % 8 == 7:
                tm = P.mark("T")
                P.wait("vector", "T", tm)
                P.do("vector", lambda e, mt=mt: e.tensor_copy(
                    vtok[:, mt - 7:mt + 1, 0:DH],
                    psB_bf[:, 2048:2560].rearrange("p (a c) -> p a c", a=8)),
                    inc="V")
        vtok_done = P.do("vector", lambda e: e.nop(), inc="V")
        P.wait("scalar", "dT", tbl_done)
        P.wait(G, "dK", P.mark("dK"))
        P.wait("tensor", "dK", P.mark("dK"))

        rb_dummy = 0

        def swv2(s, par, n):
            off = SWBASE + s * 2048 + par * 1024
            return arC.ap()[:, off:off + n]

        def etv2(s, par, a, b):
            off = ETBASE + s * 2048 + par * 1024
            return arC.ap()[:, off + a:off + b]

        def hstream(s):
            soff = s * 1024
            sgsem = ("dSG0", "dSG1") if s == 0 else ("dSG2", "dSG3")
            enl = enull_p[s]
            vnl = vnule[64:65, :]
            ksrc = kvb if s == 0 else kdup
            base = s * 64
            tasks = [(h, qh) for h in range(s, 16, 2) for qh in (0, 1)]
            swm_p = {}
            stt_h = {}
            sw_marks = {}
            rbq = {}
            rb_done_t = {}
            et_free = {}
            exp_d = {}
            odd_dma = 0

            def nsteps(ti):
                return 8 * (tasks[ti][1] + 1)

            def sw_issue(ti, m):
                h, qh = tasks[ti]
                qc0l = max(0, m // 4 - 2 * qh)
                cntl = 2 - qc0l
                par = m % 2
                key = (ti, m - 2)
                if key not in swm_p and m < 2 and ti > 0:
                    key = (ti - 1, nsteps(ti - 1) - 2 + m)
                if key in swm_p:
                    P.wait(G, "V", swm_p[key])
                off = h * 4095 + 128 * m + 1024 * (1 - qh)
                mk = P.do(G, lambda e, off=off, n=512 * cntl, par=par:
                          e.dma_start(swv2(s, par, n),
                                      bass.AP(TBLh, off, [[1, 128], [1, n]])),
                          inc=sgsem[par], by=16)
                sw_marks[(ti, m)] = (sgsem[par], mk)

            def nulls_for(ti):
                h, qh = tasks[ti]
                for l in range(2):
                    P.do("tensor", lambda e, h=h, qh=qh, l=l: e.matmul(
                        psA[64:65, soff + l * 512:soff + l * 512 + 512],
                        nulkd[base:base + 64, 0:1],
                        qT[base:base + 64, h // 2,
                           qh * 1024 + l * 512:qh * 1024 + l * 512 + 512],
                        start=True, stop=True), inc="T")
                tnn = P.mark("T")
                P.wait("scalar", "T", tnn)
                P.wait("scalar", "S", P.mark("S"))
                return P.do("scalar", lambda e, h=h, qh=qh: e.activation(
                    enl[:, qh * 1024:qh * 1024 + 1024],
                    psA[64:65, soff:soff + 1024], AF.Exp,
                    bias=nabm8[0:1, h:h + 1]), inc="S")

            P.wait("tensor", "V", vtok_done)
            snl = nulls_for(0)
            sw_issue(0, 0)
            sw_issue(0, 1)
            yield
            for ti, (h, qh) in enumerate(tasks):
                mh = nsteps(ti)
                exp_prev = snl
                exp_pp = snl
                stt_prev = P.mark("V")

                def emit_av(m, ti=ti, h=h, qh=qh):
                    qc0m = max(0, m // 4 - 2 * qh)
                    par = m % 2
                    if m == 0 and ti > 0:
                        P.wait("tensor", "V", rb_done_t[ti - 1])
                    lastm = 0
                    for l in range(qc0m, 2):
                        P.wait("tensor", "S", exp_d[(ti, m)][l])
                        lastm = P.do("tensor", lambda e, m=m, l=l, par=par:
                                     e.matmul(
                            psB[0:65, soff + l * 512:soff + l * 512 + 512],
                            vtok[:, m, :],
                            etv2(s, par, l * 512, l * 512 + 512),
                            start=(m == 0), stop=False), inc="T")
                    et_free[(ti, m)] = lastm

                for mt in range(mh):
                    qc0l = max(0, mt // 4 - 2 * qh)
                    cntl = 2 - qc0l
                    par = mt % 2
                    swsem, swm = sw_marks[(ti, mt)]
                    if mt == 0:
                        P.wait("tensor", "S", snl)
                    for l in range(qc0l, 2):
                        if mt == 0 and ti > 0:
                            P.wait("tensor", "V", rbq[(ti - 1, l)])
                        if mt > 0:
                            P.wait("tensor", "V",
                                   stt_h[(ti, mt - 1)].get(l, stt_prev))
                        P.do("tensor", lambda e, l=l, mt=mt, h=h, qh=qh:
                             e.matmul(
                            psA[:, soff + l * 512:soff + l * 512 + 512],
                            ksrc[base:base + 64, ts(mt)],
                            qT[base:base + 64, h // 2,
                               qh * 1024 + l * 512:qh * 1024 + l * 512 + 512],
                            start=True, stop=True), inc="T")
                    tsc = P.mark("T")
                    P.wait("vector", "T", tsc)
                    P.wait("vector", swsem, swm)
                    bbuf = biased if par == 0 else biased2v
                    P.wait("vector", "S", exp_pp)
                    if mt <= 1 and ti > 0:
                        P.wait("vector", "V", rb_done_t[ti - 1])
                    w = 512 * cntl
                    sw0_ = SWBASE + s * 2048 + par * 1024 + w - 1
                    vh = {}
                    for l in range(qc0l, 2):
                        wseg = 512
                        swrev = bass.AP(arC, sw0_ - (l - qc0l) * 512,
                                        [[10240, 128], [-1, wseg]])
                        vh[l] = P.do("vector", lambda e, l=l, swrev=swrev,
                                     bbuf=bbuf: e.scalar_tensor_tensor(
                            bbuf[:, soff + l * 512:soff + l * 512 + 512],
                            psA[:, soff + l * 512:soff + l * 512 + 512],
                            -CSTAB, swrev, op0=Alu.add, op1=Alu.add),
                            inc="V")
                    vbias = vh[1]
                    swm_p[(ti, mt)] = vbias
                    stt_h[(ti, mt)] = vh
                    stt_prev = vbias
                    if mt + 2 < mh:
                        sw_issue(ti, mt + 2)
                    elif ti + 1 < len(tasks):
                        sw_issue(ti + 1, mt + 2 - mh)
                    if (ti, mt - 2) in et_free:
                        P.wait("scalar", "T", et_free[(ti, mt - 2)])
                    eh = {}
                    for l in range(qc0l, 2):
                        P.wait("scalar", "V", vh[l])
                        eh[l] = P.do("scalar", lambda e, l=l, par=par,
                                     bbuf=bbuf: e.activation(
                            etv2(s, par, l * 512, l * 512 + 512),
                            bbuf[:, soff + l * 512:soff + l * 512 + 512],
                            AF.Exp), inc="S")
                    exp_pp = exp_prev
                    exp_prev = eh[1]
                    exp_d[(ti, mt)] = eh
                    if mt > 0:
                        emit_av(mt - 1)
                    if mt == mh - 1:
                        emit_av(mt)
                    if mt == 8 * qh + 4:
                        P.do("tensor", lambda e, qh=qh: e.matmul(
                            psB[0:65, soff:soff + 512], vnl,
                            enl[:, qh * 1024:qh * 1024 + 512],
                            start=False, stop=True), inc="T")
                        tq0 = P.mark("T")
                        P.wait("scalar", "T", tq0)
                        P.wait("scalar", "S", P.mark("S"))
                        P.do("scalar", lambda e: e.activation(
                            invw[64:65, soff:soff + 512],
                            psB[64:65, soff:soff + 512], AF.Ln), inc="S")
                        P.wait("scalar", "S", P.mark("S"))
                        exq0 = P.do("scalar", lambda e: e.activation(
                            invb[64:65, soff:soff + 512],
                            invw[64:65, soff:soff + 512], AF.Exp,
                            scale=-1.0), inc="S")
                    if mt == 8 * qh + 5:
                        P.wait("tensor", "S", exq0)
                        P.do("tensor", lambda e: e.matmul(
                            psA[0:64, soff:soff + 512], onesT[64:65, 0:64],
                            invb[64:65, soff:soff + 512],
                            start=True, stop=True), inc="T")
                        trb0 = P.mark("T")
                        P.wait("vector", "T", trb0)
                        P.wait("vector", "S", exp_d[(ti, mt - 2)][1])
                        if s == 1 and odd_dma:
                            P.wait("vector", "dOD", odd_dma)
                        rbq[(ti, 0)] = P.do("vector", lambda e: e.tensor_copy(
                            biased[0:64, soff:soff + 512],
                            psA[0:64, soff:soff + 512]), inc="V")
                        P.wait("vector", "V", rbq[(ti, 0)])
                        col0 = qh * 1024
                        dst0 = (optT[0:64, h // 2, col0:col0 + 512] if s == 0
                                else oddsl[0:64, col0:col0 + 512])
                        ml0 = P.do("vector", lambda e, dst0=dst0:
                                   e.tensor_tensor(
                            dst0, psB[0:64, soff:soff + 512],
                            biased[0:64, soff:soff + 512], Alu.mult),
                            inc="V")
                    yield
                s_lastexp = P.mark("S")
                P.wait("tensor", "V", stt_prev)
                P.wait("tensor", "S", s_lastexp)
                P.do("tensor", lambda e, qh=qh: e.matmul(
                    psB[0:65, soff + 512:soff + 1024],
                    vnl, enl[:, qh * 1024 + 512:qh * 1024 + 1024],
                    start=False, stop=True), inc="T")
                tav = P.mark("T")
                if ti + 1 < len(tasks):
                    snl = nulls_for(ti + 1)
                yield
                P.wait("scalar", "T", tav)
                P.wait("scalar", "S", P.mark("S"))
                P.do("scalar", lambda e: e.activation(
                    invw[64:65, soff + 512:soff + 1024],
                    psB[64:65, soff + 512:soff + 1024], AF.Ln), inc="S")
                P.wait("scalar", "S", P.mark("S"))
                exq1 = P.do("scalar", lambda e: e.activation(
                    invb[64:65, soff + 512:soff + 1024],
                    invw[64:65, soff + 512:soff + 1024],
                    AF.Exp, scale=-1.0), inc="S")
                P.wait("vector", "S", s_lastexp)
                P.wait("tensor", "S", exq1)
                P.do("tensor", lambda e: e.matmul(
                    psA[0:64, soff + 512:soff + 1024],
                    onesT[64:65, 0:64],
                    invb[64:65, soff + 512:soff + 1024],
                    start=True, stop=True), inc="T")
                trb = P.mark("T")
                P.wait("vector", "T", trb)
                rbq[(ti, 1)] = P.do("vector", lambda e: e.tensor_copy(
                    biased[0:64, soff + 512:soff + 1024],
                    psA[0:64, soff + 512:soff + 1024]), inc="V")
                P.wait("vector", "V", rbq[(ti, 1)])
                col = qh * 1024 + 512
                dstv = (optT[0:64, h // 2, col:col + 512] if s == 0
                        else oddsl[0:64, col:col + 512])
                rb_done_t[ti] = P.do(
                    "vector", lambda e, dstv=dstv: e.tensor_tensor(
                        dstv,
                        psB[0:64, soff + 512:soff + 1024],
                        biased[0:64, soff + 512:soff + 1024],
                        Alu.mult), inc="V")
                if s == 1 and qh == 1:
                    P.wait(G, "V", rb_done_t[ti])
                    odd_dma = P.do(G, lambda e, h=h: e.dma_start(
                        optT[64:128, h // 2, :], oddsl[0:64, :]),
                        inc="dOD", by=16)
                yield

        g0 = hstream(0)
        g1 = hstream(1)
        alive = [True, True]
        gens = [g0, g1]
        while alive[0] or alive[1]:
            for gi in (0, 1):
                if alive[gi]:
                    try:
                        next(gens[gi])
                    except StopIteration:
                        alive[gi] = False
        heads_done = P.mark("V")
        odd_in = P.mark("dOD")

        outg_b = cb(arB, 0, [1024])
        outb_b = cb(arB, 1024, [1024])
        olg = cf(arB, 2048, [1024])[0:1, :]
        olb = cf(arB, 6144, [1024])[0:1, :]
        wores = cb(arC, 2048, [8, 1024])
        P.wait("sync", "V", heads_done)
        P.wait("sync", "dPC", wo_pc)
        P.do("sync", lambda e: e.dma_start(olg, OutLnG), inc="dWo", by=16)
        rowm = P.do("sync", lambda e: e.dma_start(olb, OutLnB), inc="dWo",
                    by=16)
        for i in range(8):
            P.do("sync", lambda e, i=i: e.dma_start(
                wores[:, i, :], Wo_b[i]), inc="dWo", by=16)
        wo_in = P.mark("dWo")
        P.wait("tensor", "dWo", wo_in)
        P.wait("tensor", "V", heads_done)
        for c in range(2):
            P.do("tensor", lambda e, c=c: e.matmul(
                psA[:, ts(c, 512)], ones1f[:], olg[0:1, ts(c, 512)],
                start=True, stop=True), inc="T")
            P.do("tensor", lambda e, c=c: e.matmul(
                psA[:, 1024 + c * 512:1536 + c * 512], ones1f[:],
                olb[0:1, ts(c, 512)], start=True, stop=True), inc="T")
        tgb = P.mark("T")
        P.wait("vector", "T", tgb)
        for c in range(2):
            P.do("vector", lambda e, c=c: e.tensor_copy(
                outg_b[:, ts(c, 512)], psA[:, ts(c, 512)]), inc="V")
            P.do("vector", lambda e, c=c: e.tensor_copy(
                outb_b[:, ts(c, 512)], psA[:, 1024 + c * 512:1536 + c * 512]),
                inc="V")
        gb_done = P.mark("V")

        ywk = [ywork, cf(arD, 2048, [1024]), cf(arD, 4096, [1024])]
        resm = {}
        lnm = {}
        lnv5 = 0
        P.wait("tensor", "dWo", wo_in)
        P.wait("tensor", "V", gb_done)
        P.wait("tensor", "dOD", odd_in)
        for sub in range(8):
            half = sub % 2
            bs = 0 if half == 0 else 4
            pset = psA if half == 0 else psB
            for i in range(8):
                for j in range(2):
                    lt = sub * 2 + j
                    for dc in range(2):
                        b = j * 2 + dc
                        if i == 0 and sub >= 2:
                            P.wait("tensor", "V", resm[(sub - 2, j, dc)])
                        P.do("tensor", lambda e, i=i, lt=lt, dc=dc, b=b,
                             pset=pset: e.matmul(
                            pset[:, b * 512:b * 512 + 512],
                            optT[:, i, ts(lt)],
                            wores[:, i, ts(dc, 512)],
                            start=(i == 0), stop=(i == 7)), inc="T")
            twq = P.mark("T")
            P.wait("vector", "T", twq)
            for j in range(2):
                lt = sub * 2 + j
                yw = ywk[lt % 3]
                if lt >= 3:
                    P.wait("vector", "V", resm[((lt - 3) // 2, (lt - 3) % 2,
                                               1)])
                    P.wait("vector", "Gs", lnm.get(lt - 3, 0))
                P.wait("scalar", "T", twq)
                P.wait("scalar", "S", P.mark("S"))
                P.wait("scalar", "V", lnv5)
                P.do("scalar", lambda e, pset=pset, j=j: e.activation(
                    biased[:, 0:1024], pset[:, j * 1024:j * 1024 + 1024],
                    AF.Identity, scale=1.0 / 1024.0,
                    accum_out=lnscr[:, 8:9]), inc="S")
                ssq = P.do("scalar", lambda e, pset=pset, j=j: e.activation(
                    biased[:, 1024:2048], pset[:, j * 1024:j * 1024 + 1024],
                    AF.Square, accum_out=lnscr[:, 9:10]), inc="S")
                P.wait("vector", "S", ssq)
                P.do("vector", lambda e: e.tensor_scalar(
                    lnscr[:, 10:11], lnscr[:, 9:10], 1.0 / 1024.0, None,
                    op0=Alu.mult))
                v1 = P.do("vector", lambda e: e.scalar_tensor_tensor(
                    lnscr[:, 11:12], lnscr[:, 8:9], 1.0, lnscr[:, 8:9],
                    op0=Alu.mult, op1=Alu.mult), inc="V")
                P.wait("vector", "V", v1)
                v2 = P.do("vector", lambda e: e.tensor_tensor(
                    lnscr[:, 9:10], lnscr[:, 10:11], lnscr[:, 11:12],
                    Alu.subtract), inc="V")
                P.wait("scalar", "V", v2)
                s0 = P.do("scalar", lambda e: e.activation(
                    lnscr[:, 10:11], lnscr[:, 9:10], AF.Ln, bias=epsb[:],
                    scale=1.0), inc="S")
                P.wait("scalar", "S", s0)
                s1 = P.do("scalar", lambda e: e.activation(
                    lnscr[:, 11:12], lnscr[:, 10:11], AF.Exp, scale=-0.5),
                    inc="S")
                P.wait("vector", "S", s1)
                for dc in range(2):
                    b = j * 2 + dc
                    v5 = P.do("vector", lambda e, pset=pset, b=b, dc=dc,
                              yw=yw: e.tensor_scalar(
                        yw[:, dc * 512:dc * 512 + 512],
                        pset[:, b * 512:b * 512 + 512],
                        lnscr[:, 8:9], lnscr[:, 11:12],
                        op0=Alu.subtract, op1=Alu.mult), inc="V")
                lnv5 = v5
                P.wait("vector", "V", v5)
                P.wait(G, "V", v5)
                v6 = P.do("vector", lambda e, yw=yw: e.tensor_tensor(
                    yw[:, 0:512], yw[:, 0:512], outg_b[:, 0:512], Alu.mult),
                    inc="V")
                g6 = P.do(G, lambda e, yw=yw: e.tensor_tensor(
                    yw[:, 512:1024], yw[:, 512:1024], outg_b[:, 512:1024],
                    Alu.mult), inc="Gs")
                P.wait("vector", "V", v6)
                P.wait(G, "Gs", g6)
                v7 = P.do("vector", lambda e, yw=yw: e.tensor_tensor(
                    yw[:, 0:512], yw[:, 0:512], outb_b[:, 0:512], Alu.add),
                    inc="V")
                g7 = P.do(G, lambda e, yw=yw: e.tensor_tensor(
                    yw[:, 512:1024], yw[:, 512:1024], outb_b[:, 512:1024],
                    Alu.add), inc="Gs")
                P.wait("vector", "V", v7)
                resm[(sub, j, 0)] = P.do(
                    "vector", lambda e, lt=lt, yw=yw: e.tensor_tensor(
                        stream[:, lt, 0:512], stream[:, lt, 0:512],
                        yw[:, 0:512], Alu.add), inc="V")
                P.wait("vector", "Gs", g7)
                resm[(sub, j, 1)] = P.do(
                    "vector", lambda e, lt=lt, yw=yw: e.tensor_tensor(
                        stream[:, lt, 512:1024], stream[:, lt, 512:1024],
                        yw[:, 512:1024], Alu.add), inc="V")
                lnm[lt] = g7
        return P.mark("V")

    last = attn_a()
    if _PHASES >= 2:
        last = ffn("f1")
    if _PHASES >= 3:
        last = attn_b()
    if _PHASES >= 4:
        last = ffn("f2")

    P.wait("sync", "V", last)
    for s_ in ["T", "V", "S", "Gs", "dK", "dSG0", "dSG1", "dOD", "dW0",
               "dW1", "dU0", "dU1", "dT", "dPC", "dPC2", "dPC3", "dWo",
               "dR", "dP1", "dX", "dC", "dSW0", "dSW1", "dSG2", "dSG3",
               "dU2", "dU3", "dSW2", "dSW3"]:
        P.wait("sync", s_, P.mark(s_))
    if _PHASES < 4:
        for lt in range(NLT):
            P.do("sync", lambda e, lt=lt: e.dma_start(
                OUT[ts(lt), :], stream[:, lt, :]), inc="dO", by=16)
    P.wait("sync", "dO", P.mark("dO"))

    with nc.Block() as block:
        for eng in ENGS:
            def section(e, eng=eng):
                for f in P.ops[eng]:
                    f(e)
            getattr(block, eng)(section)

    stack.close()
    return nc


def _host_prep(inputs):
    f = np.float32
    gi = lambda k: np.asarray(inputs[k], f)
    w_in, w_out = gi("L0_attn_a_w_in"), gi("L0_attn_a_w_out")
    b_in, b_out = gi("L0_attn_a_b_in"), gi("L0_attn_a_b_out")
    Wq, Wkv, Wo = gi("L0_attn_b_Wq"), gi("L0_attn_b_Wkv"), gi("L0_attn_b_Wo")
    gate_W = gi("L0_ffn_gate_W")
    W1, W2 = gi("L0_ffn_W1"), gi("L0_ffn_W2")
    Ws1, Ws2 = gi("L0_ffn_Ws1"), gi("L0_ffn_Ws2")
    b1, b2 = gi("L0_ffn_b1"), gi("L0_ffn_b2")
    bs1, bs2 = gi("L0_ffn_bs1"), gi("L0_ffn_bs2")

    d = {}
    w_inT = np.ascontiguousarray(w_in.T)
    d["w_inT_t"] = np.ascontiguousarray(
        w_inT.reshape(8, 128, 24, 128).transpose(2, 1, 0, 3))
    d["b_inT"] = np.ascontiguousarray(b_in.reshape(24, 128).T)
    d["w_outT_t"] = np.ascontiguousarray(w_out.T.reshape(8, 128, 1024))
    d["b_out_row"] = b_out.reshape(1, D)
    d["gate_W_t"] = np.ascontiguousarray(
        gate_W.reshape(8, 128, 8).transpose(1, 0, 2))
    d["W1_t"] = np.ascontiguousarray(
        W1.reshape(8, 8, 128, 4, 128).transpose(0, 2, 3, 1, 4))
    d["b1T"] = np.ascontiguousarray(
        b1.reshape(8, 4, 128).transpose(2, 0, 1).reshape(128, 32))
    d["W2_t"] = np.ascontiguousarray(
        W2.reshape(8, 4, 128, 1024).transpose(0, 2, 1, 3))
    d["b2ext"] = np.ascontiguousarray(
        np.concatenate([b2, bs2.reshape(1, D)], axis=0))
    d["Ws1_t"] = np.ascontiguousarray(
        Ws1.reshape(8, 128, 4, 128).transpose(1, 2, 0, 3))
    d["bs1T"] = np.ascontiguousarray(bs1.reshape(4, 128).T)
    d["Ws2_t"] = np.ascontiguousarray(
        Ws2.reshape(4, 128, 1024).transpose(1, 0, 2))
    d["route_scale"] = gi("L0_ffn_route_scale").reshape(1, 1)
    d["ones_row"] = np.ones((1, L), f)
    d["Wq_t"] = np.ascontiguousarray(
        Wq.reshape(8, 128, 8, 128).transpose(2, 1, 0, 3))
    d["Wkv_t"] = np.ascontiguousarray(
        Wkv.reshape(8, 128, 128).transpose(1, 0, 2))
    d["lngT"] = np.ascontiguousarray(gi("L0_attn_b_ln_g").reshape(8, 128).T)
    d["lnbT"] = np.ascontiguousarray(gi("L0_attn_b_ln_b").reshape(8, 128).T)
    d["Wo_m"] = np.ascontiguousarray(Wo.reshape(8, 128, 1024))
    d["out_ln_g"] = gi("L0_attn_b_out_ln_g").reshape(1, D)
    d["out_ln_b"] = gi("L0_attn_b_out_ln_b").reshape(1, D)
    nk = gi("L0_attn_b_null_kv")
    d["null_k_dup"] = np.ascontiguousarray(
        np.concatenate([nk[0], nk[0]]).reshape(128, 1))
    d["vnull_ext"] = np.ascontiguousarray(
        np.concatenate([nk[1], np.ones(1, f)]).reshape(1, DH + 1))
    d["nab_row"] = gi("L0_attn_b_null_attn_bias").reshape(1, NHEAD)
    d["pb_w0"] = gi("L0_attn_b_pb_w0").reshape(1, D)
    d["pb_w1_t"] = np.ascontiguousarray(
        gi("L0_attn_b_pb_w1").reshape(8, 128, 8, 128).transpose(1, 2, 0, 3))
    d["pb_w2_t"] = np.ascontiguousarray(
        gi("L0_attn_b_pb_w2").reshape(8, 128, 16).transpose(1, 0, 2))
    for nm, key in [("pb_b0T", "L0_attn_b_pb_b0"),
                    ("pb_ln0_gT", "L0_attn_b_pb_ln0_g"),
                    ("pb_ln0_bT", "L0_attn_b_pb_ln0_b"),
                    ("pb_b1T", "L0_attn_b_pb_b1"),
                    ("pb_ln1_gT", "L0_attn_b_pb_ln1_g"),
                    ("pb_ln1_bT", "L0_attn_b_pb_ln1_b")]:
        d[nm] = np.ascontiguousarray(gi(key).reshape(8, 128).T)
    d["pb_b2c"] = gi("L0_attn_b_pb_b2").reshape(16, 1)
    d["pb_b0_row"] = gi("L0_attn_b_pb_b0").reshape(1, D)
    d["ident"] = np.eye(128, dtype=f)
    ps = np.zeros((8, 4, 128), f)
    for pp in range(4):
        ps[2 * pp, pp, 0:64] = 1.0
        ps[2 * pp + 1, pp, 64:128] = 1.0
    d["pairsel"] = np.ascontiguousarray(ps.reshape(8, 512))
    se = np.zeros((9, 8, 128), f)
    for e_i in range(8):
        se[e_i, e_i, :] = 1.0
    d["selE"] = np.ascontiguousarray(se.reshape(9, 1024))
    return d


def kernel(**inputs):
    shared = _host_prep(inputs)
    x = np.asarray(inputs["L0_attn_a_x"], np.float32)
    nc = build_nc()
    in_maps = []
    for core in range(NCORES):
        m = dict(shared)
        m["x"] = np.ascontiguousarray(x[core])
        in_maps.append(m)
    res = run_bass_kernel_spmd(nc, in_maps, core_ids=list(range(NCORES)))
    return np.stack([res.results[c]["out"] for c in range(NCORES)], axis=0)
